# version 06
# baseline (speedup 1.0000x reference)
import contextlib
import math
import numpy as np
import concourse.bass as bass
import concourse.mybir as mybir
from concourse.bass_utils import run_bass_kernel_spmd

F32 = mybir.dt.float32
BF16 = mybir.dt.bfloat16
FP16 = mybir.dt.float16
AF = mybir.ActivationFunctionType
ALU = mybir.AluOpType
AX = mybir.AxisListType

MEXP = 10.0
SCALE = 0.125
MAGIC = 1536.0
ENGINES = ("sync", "tensor", "vector", "scalar", "gpsimd")


class Sem:
    __slots__ = ("h", "n", "name")

    def __init__(self, h, name):
        self.h, self.n, self.name = h, 0, name


class Sched:
    def __init__(self, nc, es):
        self.nc, self.es = nc, es
        self.q = {e: [] for e in ENGINES}

    def sem(self, name):
        return Sem(self.es.enter_context(self.nc.semaphore(name)), name)

    def emit(self, eng, fn, inc=None, amt=1):
        if inc is not None:
            inc.n += amt

            def wrapped(e, fn=fn, h=inc.h, amt=amt):
                fn(e).then_inc(h, amt)
            self.q[eng].append(wrapped)
        else:
            self.q[eng].append(fn)

    def wait(self, eng, sem, val):
        assert val <= sem.n, f"{eng} waits {sem.name}>={val}, scheduled {sem.n}"
        if val > 0:
            self.q[eng].append(lambda e, h=sem.h, v=val: e.wait_ge(h, v))

    def run(self, block):
        for name in ENGINES:
            ops = self.q[name]

            def body(e, ops=ops):
                for op in ops:
                    op(e)
            getattr(block, name)(body)


def build(cfg):
    B, S, D, H, DH = cfg["B"], cfg["S"], cfg["D"], cfg["H"], cfg["DH"]
    E = 64
    NP = H // 2
    DC = D // 128
    HC = DH // 128
    NT = S // 128
    NST = S // 512
    NTC = S // 128
    assert D == H * E and S % 512 == 0 and DH % 128 == 0 and NP == DC

    nc = bass.Bass("TRN2", target_bir_lowering=False,
                   detect_race_conditions=False)

    x_in = nc.dram_tensor("x", [B, S, D], F32, kind="ExternalInput").ap()
    wq_in = nc.dram_tensor("wq", [H, D, E], F32, kind="ExternalInput").ap()
    wk_in = nc.dram_tensor("wk", [H, D, E], F32, kind="ExternalInput").ap()
    wv_in = nc.dram_tensor("wv", [H, D, E], F32, kind="ExternalInput").ap()
    bq_in = nc.dram_tensor("bq", [H, 1, E], F32, kind="ExternalInput").ap()
    bk_in = nc.dram_tensor("bk", [H, 1, E], F32, kind="ExternalInput").ap()
    bv_in = nc.dram_tensor("bv", [H, 1, E], F32, kind="ExternalInput").ap()
    wo_in = nc.dram_tensor("wo", [D, D], F32, kind="ExternalInput").ap()
    bo_in = nc.dram_tensor("bo", [D], F32, kind="ExternalInput").ap()
    w1_in = nc.dram_tensor("w1", [DH, D], F32, kind="ExternalInput").ap()
    cw_in = nc.dram_tensor("cw", [DH, 1, 3], F32, kind="ExternalInput").ap()
    cb_in = nc.dram_tensor("cb", [DH], F32, kind="ExternalInput").ap()
    w2_in = nc.dram_tensor("w2", [D, DH], F32, kind="ExternalInput").ap()
    out_d = nc.dram_tensor("out", [B, S, D], F32, kind="ExternalOutput").ap()
    DBG = cfg.get("debug", False)
    if DBG:
        d_xnf = nc.dram_tensor("d_xnf", [128, S], F32, kind="ExternalOutput").ap()
        d_u1r = nc.dram_tensor("d_u1r", [1, S + 2], F32, kind="ExternalOutput").ap()
        d_qf = nc.dram_tensor("d_qf", [128, S], F32, kind="ExternalOutput").ap()
        d_kf = nc.dram_tensor("d_kf", [128, S], F32, kind="ExternalOutput").ap()
        d_vp = nc.dram_tensor("d_vp", [128, 2048], F32, kind="ExternalOutput").ap()
        d_att = nc.dram_tensor("d_att", [128, S], F32, kind="ExternalOutput").ap()
        d_acc = nc.dram_tensor("d_acc", [128, S], F32, kind="ExternalOutput").ap()
        d_t1t = nc.dram_tensor("d_t1t", [HC, DC, 128, 128], BF16, kind="ExternalOutput").ap()
        d_xq1 = nc.dram_tensor("d_xq1", [DC, 128, S], BF16, kind="ExternalOutput").ap()
        d_s2b = nc.dram_tensor("d_s2b", [128, 512], F32, kind="ExternalOutput").ap()
        d_a2 = nc.dram_tensor("d_a2", [128, 512], F32, kind="ExternalOutput").ap()
        d_xq2 = nc.dram_tensor("d_xq2", [128, 512], BF16, kind="ExternalOutput").ap()
        d_mw = nc.dram_tensor("d_mw", [128, 4], F32, kind="ExternalOutput").ap()
        d_cols = nc.dram_tensor("d_cols", [128, 8], F32, kind="ExternalOutput").ap()
        d_xnt = nc.dram_tensor("d_xnt", [128, D], F32, kind="ExternalOutput").ap()
        d_xq1i = nc.dram_tensor("d_xq1i", [128, D], F32, kind="ExternalOutput").ap()
        d_ident = nc.dram_tensor("d_ident", [128, 128], F32, kind="ExternalOutput").ap()
        d_colh = nc.dram_tensor("d_colh", [128, 16 * NT], F32, kind="ExternalOutput").ap()
    t1t_d = nc.dram_tensor("t1t", [HC, DC, 128, 128], BF16, kind="Internal").ap()
    t2t_d = nc.dram_tensor("t2t", [DC, HC, 128, 128], BF16, kind="Internal").ap()
    xq1_d = nc.dram_tensor("xq1s", [DC, 128, S], BF16, kind="Internal").ap()

    es = contextlib.ExitStack()
    with es:
        def sb(name, shape, dt=F32):
            return es.enter_context(nc.sbuf_tensor(name, shape, dt))

        regA = sb("regA", [128, S * NP])
        regA16 = regA.bitcast(BF16)
        regB = sb("regB", [128, max(S * DC, 512 * HC, 12 * 1024)])
        regB16 = regB.bitcast(FP16)
        regC = sb("regC", [128, 8192])
        regC16 = regC.bitcast(BF16)
        regCf = regC.bitcast(FP16)
        regS = sb("regS", [128, 5152])
        regS16 = regS.bitcast(BF16)

        QF, KF, VP = 0, S, 2 * S
        PT = [2 * S + 2048, 2 * S + 2560, 2 * S + 3072, 2 * S + 3584]
        assert PT[3] + 512 <= 8192
        WQKV = [0, 0]
        XQS = 0
        T1S = [DC * 514 + 16, DC * 514 + 16 + DC * 128]
        T2S = T1S[1] + DC * 128
        assert T2S + HC * 128 <= 2 * 5152
        assert 3 * D <= 5152

        ident = sb("ident", [128, 128])
        ones = sb("ones", [128, 128])
        kc = sb("kc", [128, 8])
        rowi = sb("rowi", [128, 1])
        bqp = sb("bqp", [128, NP])
        bkp = sb("bkp", [128, NP])
        bvp = sb("bvp", [128, DC])
        bop = sb("bop", [128, DC])
        obias = sb("obias", [128, DC])
        cw0 = sb("cw0", [128, HC])
        cw1 = sb("cw1", [128, HC])
        cw2 = sb("cw2", [128, HC])
        cbt = sb("cbt", [128, HC])
        mwc = sb("mwc", [128, 2])
        mwu = sb("mwu", [128, 2])
        wacc = sb("wacc", [128, max(HC, DC)])
        wsum = sb("wsum", [128, 1])
        wsc = sb("wsc", [128, 4])
        cols = sb("cols", [128, 32])
        u1row = sb("u1row", [1, S + 2])
        zbs = sb("zbs", [128, 512])
        hsb = sb("hsb", [128, 514])
        gsb = sb("gsb", [128, 512])
        accmin = sb("accmin", [128, 512])
        amaxb = sb("amaxb", [128, 512])
        s2b = sb("s2b", [128, 512])
        u2b = sb("u2b", [128, 512])
        u1x = sb("u1x", [128, 516])
        coli = gsb[:, 0:128]
        zl = s2b
        accmax = zbs
        u1b = u1x[:, 0:514]
        m16f = u1x.bitcast(FP16)[:, 0:512]
        xqst = u1x.bitcast(BF16)[:, 0:D]
        colh = sb("colh", [128, 16 * (S // 128)]) if cfg.get("debug") else None

        ps = [es.enter_context(nc.psum_tensor(f"ps{i}", [128, 512], F32))
              for i in range(8)]

        sch = Sched(nc, es)
        s_dl = sch.sem("s_dl")
        s_st = sch.sem("s_st")
        s_pe = sch.sem("s_pe")
        s_ac = sch.sem("s_ac")
        s_dv = sch.sem("s_dv")
        s_gp = sch.sem("s_gp")
        s_lx = sch.sem("s_lx")
        s_lq = sch.sem("s_lq")
        s_xq = sch.sem("s_xq")
        s_t2 = sch.sem("s_t2")
        s_pw = [sch.sem("s_pw0"), sch.sem("s_pw1")]
        s_so = [sch.sem("s_so0"), sch.sem("s_so1")]
        s_wo = [sch.sem("s_wo0"), sch.sem("s_wo1")]
        s_t1 = [sch.sem("s_t10"), sch.sem("s_t11")]

        nc.const_aps.register(F32, -MEXP, kc[:, 0:1])
        nc.const_aps.register(F32, SCALE, kc[:, 1:2])
        nc.const_aps.register(F32, -1.0, kc[:, 2:3])
        nc.const_aps.register(F32, 1e-5, kc[:, 3:4])
        SQS = 1.0 / math.sqrt(D)
        assert (1.0 / SQS) == int(1.0 / SQS)
        nc.const_aps.register(F32, SQS, kc[:, 4:5])
        nc.const_aps.register(F32, MAGIC, kc[:, 5:6])

        def vspace(n=8):
            for _ in range(n):
                sch.emit("vector", lambda e: e.tensor_copy(
                    kc[:, 7:8], kc[:, 0:1]))

        def dma(dst, src, inc, slow=False):
            def f(e, dst=dst, src=src, slow=slow):
                return e.dma_start(out=dst, in_=src,
                                   allow_slow_non_contiguous=slow)
            sch.emit("sync", f, inc=inc, amt=16)

        def xnf(c):
            return regB[:, S * c:S * (c + 1)]

        def a2v(c):
            return regB[:, 512 * c:512 * (c + 1)]

        def attv(j):
            return regA[:, S * j:S * (j + 1)]

        def woTv(j):
            return regC[:, 1024 * j:1024 * (j + 1)]

        def xq2v(c):
            return regC16[:, 512 * c:512 * (c + 1)]

        sch.emit("gpsimd", lambda e: e.iota(
            rowi[:, :], [[1, 1]], channel_multiplier=1,
            allow_small_or_imprecise_dtypes=True))
        sch.emit("gpsimd", lambda e: e.iota(
            coli[:, :], [[1, 128]], channel_multiplier=0,
            allow_small_or_imprecise_dtypes=True), inc=s_gp)
        gp_iota = s_gp.n

        sch.emit("vector", lambda e: e.memset(ones[:, :], 1.0))
        sch.emit("vector", lambda e: e.memset(kc[:, 0:1], -MEXP))
        sch.emit("vector", lambda e: e.memset(kc[:, 1:2], SCALE))
        sch.emit("vector", lambda e: e.memset(kc[:, 2:3], -1.0))
        sch.emit("vector", lambda e: e.memset(kc[:, 3:4], 1e-5))
        sch.emit("vector", lambda e: e.memset(kc[:, 4:5], SQS))
        sch.emit("vector", lambda e: e.memset(kc[:, 5:6], MAGIC))
        sch.emit("vector", lambda e: e.memset(u1row[0:1, 0:1], 0.0))
        sch.emit("vector", lambda e: e.memset(u1row[0:1, S + 1:S + 2], 0.0))
        sch.wait("vector", s_gp, gp_iota)
        sch.emit("vector", lambda e: e.tensor_scalar(
            ident[:, :], coli[:, :], rowi[:, 0:1], None, ALU.is_equal),
            inc=s_dv)
        dv_const = s_dv.n

        dma(bqp[:, 0:NP], bass.AP(bq_in.tensor, 0, [[1, 128], [128, NP]]),
            s_dl, slow=True)
        dma(bkp[:, 0:NP], bass.AP(bk_in.tensor, 0, [[1, 128], [128, NP]]),
            s_dl, slow=True)
        dma(bvp[:, 0:DC], bass.AP(bv_in.tensor, 0, [[1, 128], [128, DC]]),
            s_dl, slow=True)
        dma(bop[:, 0:DC], bass.AP(bo_in.tensor, 0, [[1, 128], [128, DC]]),
            s_dl, slow=True)
        for k, t in ((0, cw0), (1, cw1), (2, cw2)):
            dma(t[:, 0:HC], bass.AP(cw_in.tensor, k, [[3, 128], [384, HC]]),
                s_dl, slow=True)
        dma(cbt[:, 0:HC], bass.AP(cb_in.tensor, 0, [[1, 128], [128, HC]]),
            s_dl, slow=True)
        dl_smalls = s_dl.n

        wq16 = regA16

        def quant_weight(w_ap, nch, ncol, mi, scr):
            nblk = ncol // 128
            gq = regB[:, 0:ncol]
            gm16 = regB16[:, 2 * ncol:3 * ncol]
            sb0 = ncol + ncol // 2
            stg = [regB[:, sb0 + ncol * k:sb0 + ncol * (k + 1)]
                   for k in range(2)]
            dvm = []
            for ch in range(nch):
                buf = stg[ch % 2]
                if ch >= 2:
                    sch.wait("sync", s_dv, dvm[ch - 2])
                dma(buf, w_ap[128 * ch:128 * (ch + 1), :], s_pw[ch % 2])
                sch.wait("vector", s_pw[ch % 2], s_pw[ch % 2].n)
                if ch == 0:
                    sch.wait("vector", s_dv, dv_const)
                sch.emit("vector", lambda e, b=buf, ch=ch:
                         e.scalar_tensor_tensor(
                             gq, b, -1.0, b, ALU.mult, ALU.max,
                             accum_out=wacc[:, ch:ch + 1]), inc=s_dv)
                dvm.append(s_dv.n)
            for _sp in range(8):
                sch.emit("vector", lambda e: e.tensor_copy(
                    wsc[:, 2:3], kc[:, 0:1]))
            sch.emit("vector", lambda e, nch=nch: e.reduce_sum(
                wsum[:, 0:1], wacc[:, 0:nch], axis=AX.X), inc=s_dv)
            sch.wait("vector", s_dv, s_dv.n)
            sch.wait("tensor", s_dv, s_dv.n)
            sch.emit("tensor", lambda e: e.matmul(
                ps[0][0:1, 0:1], wsum[:, 0:1], ones[:, 0:1],
                start=True, stop=True), inc=s_pe)
            sch.wait("vector", s_pe, s_pe.n)
            inv_n = 1.0 / (nch * 128 * ncol)
            sch.emit("vector", lambda e, inv_n=inv_n: e.tensor_scalar(
                wsc[0:1, 0:1], ps[0][0:1, 0:1], inv_n, 1e-5,
                ALU.mult, ALU.max), inc=s_dv)
            sch.wait("tensor", s_dv, s_dv.n)
            sch.emit("tensor", lambda e: e.matmul(
                ps[1][0:128, 0:1], ones[0:1, 0:128], wsc[0:1, 0:1],
                start=True, stop=True), inc=s_pe)
            sch.wait("vector", s_pe, s_pe.n)
            vspace()
            sch.emit("vector", lambda e, mi=mi: e.reciprocal(
                mwc[:, mi:mi + 1], ps[1][0:128, 0:1]))
            sch.emit("vector", lambda e, mi=mi: e.tensor_scalar(
                mwu[:, mi:mi + 1], ps[1][0:128, 0:1], 1.0 / 127.0, None,
                ALU.mult))
            vspace()
            sch.emit("vector", lambda e: e.tensor_copy(
                wsc[:, 3:4], mwc[:, 0:1]), inc=s_dv)
            dv_scale = s_dv.n
            dvq = []
            last_st = 0
            for ch in range(nch):
                buf = stg[ch % 2]
                if ch >= 2:
                    sch.wait("sync", s_dv, dvq[ch - 2])
                dma(buf, w_ap[128 * ch:128 * (ch + 1), :], s_pw[ch % 2])
                sch.wait("vector", s_pw[ch % 2], s_pw[ch % 2].n)
                if ch == 0:
                    sch.wait("vector", s_dv, dv_scale)
                sch.emit("vector", lambda e, b=buf, mi=mi: e.tensor_scalar(
                    gm16, b, mwc[:, mi:mi + 1], MAGIC, ALU.mult, ALU.add))
                sch.emit("vector", lambda e: e.tensor_scalar(
                    gq, gm16, -MAGIC, -1.0, ALU.add, ALU.max))
                sch.emit("vector", lambda e, b=buf: e.tensor_scalar(
                    b, gq, 1.0, None, ALU.min), inc=s_dv)
                dvq.append(s_dv.n)
                sch.wait("tensor", s_dv, s_dv.n)
                sch.wait("tensor", s_st, last_st)
                for g in range(0, nblk, 4):
                    gn = min(4, nblk - g)
                    if g > 0:
                        sch.wait("tensor", s_ac, s_ac.n)
                    for blk in range(g, g + gn):
                        sch.emit("tensor", lambda e, b=buf, blk=blk:
                                 e.transpose(ps[blk % 4][0:128, 0:128],
                                             b[:, 128 * blk:128 * (blk + 1)],
                                             ident[:, :]),
                                 inc=(s_pe if blk == g + gn - 1 else None))
                    sch.wait("scalar", s_pe, s_pe.n)
                    for blk in range(g, g + gn):
                        sch.emit("scalar", lambda e, blk=blk: e.copy(
                            wq16[:, 128 * blk:128 * (blk + 1)],
                            ps[blk % 4][0:128, 0:128]), inc=s_ac)
                sch.wait("sync", s_ac, s_ac.n)
                dma(scr[ch, :, :, :].rearrange("n p f -> p n f"),
                    wq16[:, 0:128 * nblk].rearrange("p (n f) -> p n f", f=128),
                    s_st)
                last_st = s_st.n

        quant_weight(w1_in, HC, D, 0, t1t_d)
        quant_weight(w2_in, DC, DH, 1, t2t_d)
        if DBG:
            sch.wait("sync", s_dv, s_dv.n)
            dma(d_mw[:, 0:2], mwc[:, 0:2], s_st)
            dma(d_mw[:, 2:4], mwu[:, 0:2], s_st)
            dma(d_ident, ident[:, :], s_st)
        prep_st = s_st.n
        prep_pe = s_pe.n

        regA_pe_free = 0
        regB_dv_free = s_dv.n
        regC_pe_free = prep_pe
        regC_dv_free = s_dv.n
        obias_done = False
        attev = 0
        zbev = 0
        lnv = 0
        p0_ev = s_ac.n
        oproj_ac = s_ac.n
        bl1pe = [0]
        bl2pe = [0]
        expev = {}
        blev = {}
        bl2ev = {}
        q2dv = {}
        fin_st = [0]
        fin_dv = 0

        for b in range(B):
            xnt = regC[:, 2 * D:3 * D]
            xq1i = regC[:, 3 * D:4 * D]
            m16p = regCf[:, 8 * D:9 * D]
            sch.wait("sync", s_pe, regC_pe_free)
            sch.wait("sync", s_dv, regC_dv_free)
            prev_spill = 0
            prev_pet = 0
            xt_rd = {}
            sq_ac = {}
            c1_rd = {}
            pet_h = {}
            sp_h = {0: 0, -1: 0}
            pend_t = None

            def emit_evacs(te):
                nonlocal p0_ev, prev_spill
                sch.wait("scalar", s_pe, pet_h[te])
                sch.wait("scalar", s_st, sp_h[te - 1])
                if b > 0 and te == 0:
                    sch.wait("scalar", s_dv, regB_dv_free)
                te0 = te * 128
                for c in range(DC):
                    sch.emit("scalar", lambda e, c=c, te0=te0: e.copy(
                        xnf(c)[:, te0:te0 + 128],
                        ps[c % 2][0:128, 128 * (c // 2):128 * (c // 2) + 128]))
                for c in range(DC):
                    sch.emit("scalar", lambda e, c=c: e.copy(
                        xqst[:, 128 * c:128 * (c + 1)],
                        ps[2 + c % 2][0:128,
                                      128 * (c // 2):128 * (c // 2) + 128]),
                        inc=(s_ac if c == DC - 1 else None))
                if te % 4 == 3 or te == NT - 1:
                    base = (te // 4) * 512
                    ncol = 128 * (te % 4 + 1)
                    sch.emit("scalar", lambda e, base=base, ncol=ncol: e.copy(
                        u1row[0:1, 1 + base:1 + base + ncol],
                        ps[4][0:1, 0:ncol]), inc=s_ac)
                p0_ev = s_ac.n
                sch.wait("sync", s_ac, p0_ev)

                def sp_xq(e, te0=te0):
                    outap = bass.AP(xq1_d.tensor, te0,
                                    [[S, 128], [128 * S, DC], [1, 128]])
                    return e.dma_start(
                        out=outap,
                        in_=xqst[:, 0:D].rearrange("p (n f) -> p n f", f=128))
                sch.emit("sync", sp_xq, inc=s_st, amt=16)
                sp_h[te] = s_st.n
                prev_spill = sp_h[te]

            for t in range(NT):
                t0 = t * 128
                co = 16 * (t % 2)
                colv = cols[:, co:co + 16]
                xt = regC[:, 0:D] if t % 2 == 0 else regC[:, 6 * D:7 * D]
                x2t = regC[:, D:2 * D] if t % 2 == 0 else regC[:, 5 * D:6 * D]
                if t >= 2:
                    sch.wait("sync", s_dv, xt_rd[t - 2])
                    sch.wait("sync", s_ac, sq_ac[t - 2])
                dma(xt, x_in[b, t0:t0 + 128, :], s_lx)
                dlx = s_lx.n
                sch.wait("scalar", s_lx, dlx)
                if b == 0 and t == 0:
                    sch.wait("scalar", s_dv, dv_const)
                if t >= 2:
                    sch.wait("scalar", s_dv, c1_rd[t - 2])
                sch.emit("scalar", lambda e, x2t=x2t, xt=xt: e.activation(
                    x2t, xt, AF.Square, scale=SQS), inc=s_ac)
                sq_ac[t] = s_ac.n
                acx = sq_ac[t]
                if pend_t is not None:
                    emit_evacs(pend_t)
                    pend_t = None
                sch.wait("vector", s_lx, dlx)
                sch.emit("vector", lambda e, colv=colv, xt=xt: e.reduce_sum(
                    colv[:, 0:1], xt, axis=AX.X), inc=s_dv)
                sch.wait("vector", s_dv, s_dv.n)
                sch.emit("vector", lambda e, colv=colv: e.tensor_tensor(
                    colv[:, 3:4], colv[:, 0:1], colv[:, 0:1], ALU.mult))
                sch.wait("vector", s_ac, acx)
                sch.emit("vector", lambda e, colv=colv, x2t=x2t: e.reduce_sum(
                    colv[:, 1:2], x2t, axis=AX.X), inc=s_dv)
                c1_rd[t] = s_dv.n
                sch.wait("vector", s_dv, s_dv.n)
                sch.emit("vector", lambda e, colv=colv: e.tensor_scalar(
                    colv[:, 2:3], colv[:, 0:1], 1.0 / D, None, ALU.mult))
                sch.emit("vector", lambda e, colv=colv: e.scalar_tensor_tensor(
                    colv[:, 4:5], colv[:, 3:4], -1.0 / (D * D), colv[:, 1:2],
                    ALU.mult, ALU.add), inc=s_dv)
                sch.wait("scalar", s_dv, s_dv.n)
                sch.emit("scalar", lambda e, colv=colv: e.activation(
                    colv[:, 5:6], colv[:, 4:5], AF.Sqrt, bias=1e-5), inc=s_ac)
                sch.wait("vector", s_ac, s_ac.n)
                vspace()
                sch.emit("vector", lambda e, colv=colv: e.reciprocal(
                    colv[:, 6:7], colv[:, 5:6]))
                vspace()
                sch.wait("vector", s_pe, prev_pet)
                sch.emit("vector", lambda e, colv=colv, xt=xt: e.tensor_scalar(
                    xnt, xt, colv[:, 2:3], colv[:, 6:7],
                    ALU.subtract, ALU.mult))
                sch.emit("vector", lambda e, colv=colv: e.reduce_max(
                    colv[:, 8:9], xnt, axis=AX.X, apply_absolute_value=True),
                    inc=s_dv)
                xt_rd[t] = s_dv.n
                sch.wait("vector", s_dv, s_dv.n)
                sch.emit("vector", lambda e, colv=colv: e.tensor_scalar(
                    colv[:, 9:10], colv[:, 8:9], 1e-5, None, ALU.max))
                vspace()
                sch.emit("vector", lambda e, colv=colv: e.reciprocal(
                    colv[:, 10:11], colv[:, 9:10]))
                vspace()
                sch.emit("vector", lambda e, colv=colv: e.tensor_scalar(
                    colv[:, 11:12], colv[:, 10:11], 127.0, None, ALU.mult))
                sch.emit("vector", lambda e, colv=colv: e.tensor_tensor(
                    colv[:, 12:13], colv[:, 9:10], mwu[:, 0:1], ALU.mult))
                vspace(4)
                sch.emit("vector", lambda e, colv=colv: e.tensor_scalar(
                    m16p, xnt, colv[:, 11:12], MAGIC, ALU.mult, ALU.add))
                sch.emit("vector", lambda e: e.tensor_scalar(
                    xq1i, m16p, -MAGIC, None, ALU.add), inc=s_dv)
                dvq1 = s_dv.n
                if DBG and b == 0:
                    sch.emit("vector", lambda e, t=t, colv=colv: e.tensor_copy(
                        colh[:, 16 * t:16 * (t + 1)], colv), inc=s_dv)
                    dvq1 = s_dv.n
                if DBG and b == 0 and t == 0:
                    sch.wait("sync", s_dv, dvq1)
                    dma(d_cols, colv[:, 0:8], s_st)
                    dma(d_xnt, xnt, s_st)
                    dma(d_xq1i, xq1i, s_st)
                    sch.wait("tensor", s_st, s_st.n)
                sch.wait("tensor", s_dv, dvq1)
                if b == 0 and t == 0:
                    sch.wait("tensor", s_st, prep_st)
                if b > 0 and t == 0:
                    sch.wait("tensor", s_dv, fin_dv)
                sch.wait("tensor", s_ac, p0_ev)
                for c in range(DC):
                    sch.emit("tensor", lambda e, c=c: e.transpose(
                        ps[c % 2][0:128, 128 * (c // 2):128 * (c // 2) + 128],
                        xnt[:, 128 * c:128 * (c + 1)], ident[:, :]))
                for c in range(DC):
                    sch.emit("tensor", lambda e, c=c: e.transpose(
                        ps[2 + c % 2][0:128, 128 * (c // 2):128 * (c // 2) + 128],
                        xq1i[:, 128 * c:128 * (c + 1)], ident[:, :]))
                sch.emit("tensor", lambda e, t=t, colv=colv: e.transpose(
                    ps[4][0:1, 128 * (t % 4):128 * (t % 4) + 128],
                    colv[:, 12:13], ident[:, :]), inc=s_pe)
                pet = s_pe.n
                prev_pet = pet
                pet_h[t] = pet
                pend_t = t
            emit_evacs(NT - 1)
            p0_spill = s_st.n
            p0_dv = s_dv.n
            regC_pe_free = s_pe.n
            if DBG and b == 0:
                sch.wait("sync", s_st, p0_spill)
                dma(d_xnf, xnf(0), s_st)
                dma(d_u1r, u1row[0:1, :], s_st)
                dma(d_xq1, xq1_d[:, :, :].rearrange("n p f -> p n f"), s_st)
                dma(d_colh, colh[:, :], s_st)
                dma(d_t1t, t1t_d[:, :, :, :].rearrange("a n p f -> p a n f"),
                    s_st)

            dl_wqkv = {}

            def load_pair(p):
                buf = WQKV[p % 2]
                for wi, w_ap in ((0, wq_in), (1, wk_in), (2, wv_in)):
                    for two in range(2):
                        def ldw(e, p=p, wi=wi, w_ap=w_ap, buf=buf, two=two):
                            inap = bass.AP(w_ap.tensor,
                                           (2 * p + two) * D * E,
                                           [[E, 128], [128 * E, DC], [1, E]])
                            outap = bass.AP(regS, buf + D * wi + 64 * two,
                                            [[5152, 128], [128, DC], [1, 64]])
                            return e.dma_start(out=outap, in_=inap)
                        sch.emit("sync", ldw, inc=s_lq, amt=16)
                dl_wqkv[p] = s_lq.n

            sch.wait("sync", s_pe, bl2pe[-1])
            vpe_hist = {}
            for p in range(NP):
                buf = WQKV[p % 2]
                qk = regS[:, buf:buf + D]
                kk = regS[:, buf + D:buf + 2 * D]
                vk = regS[:, buf + 2 * D:buf + 3 * D]
                if p > 0:
                    sch.wait("sync", s_pe, vpe_hist[p - 1])
                load_pair(p)
                sch.wait("tensor", s_lq, dl_wqkv[p])
                sch.wait("tensor", s_ac, zbev if p > 0 else p0_ev)
                sch.wait("tensor", s_dv, attev if p > 0 else
                         (fin_dv if b > 0 else 0))
                if p == 0 and b > 0:
                    sch.wait("vector", s_pe, regA_pe_free)
                for st in range(NST):
                    bank = st % 4
                    for dc in range(DC):
                        sch.emit("tensor", lambda e, st=st, dc=dc, qk=qk, bank=bank:
                                 e.matmul(ps[bank][0:128, 0:512],
                                          qk[:, 128 * dc:128 * (dc + 1)],
                                          xnf(dc)[:, 512 * st:512 * (st + 1)],
                                          start=(dc == 0), stop=(dc == DC - 1)),
                                 inc=(s_pe if dc == DC - 1 else None))
                qpe = s_pe.n
                for st in range(NST):
                    bank = 4 + st % 4
                    for dc in range(DC):
                        sch.emit("tensor", lambda e, st=st, dc=dc, kk=kk, bank=bank:
                                 e.matmul(ps[bank][0:128, 0:512],
                                          kk[:, 128 * dc:128 * (dc + 1)],
                                          xnf(dc)[:, 512 * st:512 * (st + 1)],
                                          start=(dc == 0), stop=(dc == DC - 1)),
                                 inc=(s_pe if dc == DC - 1 else None))
                kpe = s_pe.n
                sch.wait("scalar", s_pe, qpe)
                if p == 0:
                    sch.wait("scalar", s_dl, dl_smalls)

                for st in range(NST):
                    sch.emit("scalar", lambda e, st=st, p=p: e.activation(
                        regC[:, QF + 512 * st:QF + 512 * (st + 1)],
                        ps[st % 4][0:128, 0:512], AF.Identity,
                        bias=bqp[:, p:p + 1]),
                        inc=(s_ac if st == NST - 1 else None))
                qev = s_ac.n
                sch.wait("scalar", s_pe, kpe)
                for st in range(NST):
                    sch.emit("scalar", lambda e, st=st, p=p: e.activation(
                        regC[:, KF + 512 * st:KF + 512 * (st + 1)],
                        ps[4 + st % 4][0:128, 0:512], AF.Identity,
                        bias=bkp[:, p:p + 1]),
                        inc=(s_ac if st == NST - 1 else None))
                kev = s_ac.n
                sch.wait("tensor", s_ac, qev)
                for tc in range(NTC):
                    bank = tc % 4
                    for dc in range(DC):
                        sch.emit("tensor",
                                 lambda e, tc=tc, dc=dc, vk=vk, bank=bank:
                                 e.matmul(
                                     ps[bank][0:128,
                                              128 * (tc // 4):128 * (tc // 4) + 128],
                                     xnf(dc)[:, 128 * tc:128 * (tc + 1)],
                                     vk[:, 128 * dc:128 * (dc + 1)],
                                     start=(dc == 0), stop=(dc == DC - 1)),
                                 inc=(s_pe if (dc == DC - 1 and tc >= NTC - 4)
                                      else None))
                vpe = s_pe.n
                vpe_hist[p] = vpe
                sch.wait("scalar", s_pe, vpe)
                for bank in range(4):
                    ntc_b = (NTC - 1 - bank) // 4 + 1
                    sch.emit("scalar", lambda e, bank=bank, ntc_b=ntc_b: e.copy(
                        bass.AP(regC, VP + 128 * bank,
                                [[8192, 128], [512, ntc_b], [1, 128]]),
                        ps[bank][0:128, 0:128 * ntc_b]
                        .rearrange("p (n f) -> p n f", f=128)),
                        inc=(s_ac if bank == 3 else None))
                vev = s_ac.n
                for st in range(NST):
                    sch.wait("tensor", s_dv, attev)
                    sch.wait("tensor", s_ac, lnv)
                    for tc in range(NTC):
                        par = tc % 2
                        bA, bB = 2 * par, 2 * par + 1
                        ptA = regC[:, PT[par]:PT[par] + 512]
                        ptB = regC[:, PT[2 + par]:PT[2 + par] + 512]
                        if tc <= 1:
                            g = vev
                            if st > 0:
                                g = max(g, expev[(p, st - 1, NTC - 2 + par)])
                            if par == 1:
                                g = max(g, zbev)
                            sch.wait("tensor", s_ac, g)
                        else:
                            sch.wait("tensor", s_ac, expev[(p, st, tc - 2)])
                        sch.emit("tensor", lambda e, tc=tc, st=st, bA=bA:
                                 e.matmul(ps[bA][0:128, 0:512],
                                          regC[0:64, KF + 128 * tc:KF + 128 * (tc + 1)],
                                          regC[0:64, QF + 512 * st:QF + 512 * (st + 1)],
                                          start=True, stop=True), inc=s_pe)
                        sch.emit("tensor", lambda e, tc=tc, st=st, bB=bB:
                                 e.matmul(ps[bB][0:128, 0:512],
                                          regC[64:128, KF + 128 * tc:KF + 128 * (tc + 1)],
                                          regC[64:128, QF + 512 * st:QF + 512 * (st + 1)],
                                          start=True, stop=True), inc=s_pe)
                        stpe = s_pe.n
                        sch.wait("scalar", s_pe, stpe)
                        sch.emit("scalar", lambda e, bA=bA, ptA=ptA: e.activation(
                            ptA, ps[bA][0:128, 0:512], AF.Exp,
                            bias=-MEXP, scale=SCALE))
                        sch.emit("scalar", lambda e, bB=bB, ptB=ptB: e.activation(
                            ptB, ps[bB][0:128, 0:512], AF.Exp,
                            bias=-MEXP, scale=SCALE), inc=s_ac)
                        expev[(p, st, tc)] = s_ac.n
                        sch.wait("tensor", s_ac, expev[(p, st, tc)])
                        stt = (tc == 0)
                        stp = (tc == NTC - 1)
                        sch.emit("tensor", lambda e, tc=tc, ptA=ptA, stt=stt, stp=stp:
                                 e.matmul(ps[4][0:64, 0:512],
                                          regC[:, VP + 128 * tc:VP + 128 * tc + 64],
                                          ptA, start=stt, stop=stp,
                                          tile_position=(0, 0)),
                                 inc=(s_pe if stp else None))
                        sch.emit("tensor", lambda e, tc=tc, ptB=ptB, stt=stt, stp=stp:
                                 e.matmul(ps[5][64:128, 0:512],
                                          regC[:, VP + 128 * tc + 64:VP + 128 * (tc + 1)],
                                          ptB, start=stt, stop=stp,
                                          tile_position=(0, 64)),
                                 inc=(s_pe if stp else None))
                        sch.emit("tensor", lambda e, ptA=ptA, stt=stt, stp=stp:
                                 e.matmul(ps[6][0:1, 0:512], ones[:, 0:1],
                                          ptA, start=stt, stop=stp,
                                          tile_position=(0, 0)),
                                 inc=(s_pe if stp else None))
                        sch.emit("tensor", lambda e, ptB=ptB, stt=stt, stp=stp:
                                 e.matmul(ps[7][32:33, 0:512], ones[:, 0:1],
                                          ptB, start=stt, stop=stp,
                                          tile_position=(0, 32)),
                                 inc=(s_pe if stp else None))
                    pvz = s_pe.n
                    sch.wait("scalar", s_pe, pvz)
                    sch.emit("scalar", lambda e: e.activation(
                        zl[0:1, 0:512], ps[6][0:1, 0:512], AF.Ln))
                    sch.emit("scalar", lambda e: e.activation(
                        zl[32:33, 0:512], ps[7][32:33, 0:512], AF.Ln),
                        inc=s_ac)
                    lnv = s_ac.n
                    sch.wait("tensor", s_ac, lnv)
                    sch.wait("tensor", s_ac, zbev)
                    sch.emit("tensor", lambda e: e.matmul(
                        ps[2][0:64, 0:512], ones[0:1, 0:64], zl[0:1, 0:512],
                        start=True, stop=True))
                    sch.emit("tensor", lambda e: e.matmul(
                        ps[2][64:128, 0:512], ones[32:33, 0:64],
                        zl[32:33, 0:512], start=True, stop=True,
                        tile_position=(32, 64)), inc=s_pe)
                    zbpe = s_pe.n
                    sch.wait("scalar", s_pe, zbpe)
                    sch.wait("scalar", s_dv, attev)
                    sch.emit("scalar", lambda e: e.activation(
                        zbs[:, 0:512], ps[2][0:128, 0:512], AF.Exp,
                        scale=-1.0), inc=s_ac)
                    zbev = s_ac.n
                    sch.wait("vector", s_ac, zbev)
                    sch.emit("vector", lambda e, p=p, st=st: e.tensor_tensor(
                        attv(p)[0:64, 512 * st:512 * (st + 1)],
                        ps[4][0:64, 0:512], zbs[0:64, 0:512], ALU.mult))
                    sch.emit("vector", lambda e, p=p, st=st: e.tensor_tensor(
                        attv(p)[64:128, 512 * st:512 * (st + 1)],
                        ps[5][64:128, 0:512], zbs[64:128, 0:512], ALU.mult),
                        inc=s_dv)
                    attev = s_dv.n

            if DBG and b == 0:
                sch.wait("sync", s_ac, s_ac.n)
                sch.wait("sync", s_dv, attev)
                dma(d_qf, regC[:, QF:QF + S], s_st)
                dma(d_kf, regC[:, KF:KF + S], s_st)
                dma(d_vp, regC[:, VP:VP + 2048], s_st)
                dma(d_att, attv(0), s_st)
                sch.wait("tensor", s_st, s_st.n)

            wst = [regB[:, 0:D], regB[:, D:2 * D]]
            wo_tr = {}
            tr_ac = 0
            for ih in range(DC):
                bufw = wst[ih % 2]
                sch.wait("sync", s_pe, wo_tr.get(ih - 2, vpe_hist[NP - 1]))
                if ih < 2:
                    sch.wait("sync", s_pe, vpe_hist[NP - 1])
                dma(bufw, wo_in[128 * ih:128 * (ih + 1), :], s_wo[ih % 2])
                sch.wait("tensor", s_wo[ih % 2], s_wo[ih % 2].n)
                if ih == 0:
                    sch.wait("tensor", s_ac, zbev)
                    sch.wait("tensor", s_dv, attev)
                for g in range(0, DC, 4):
                    gn = min(4, DC - g)
                    sch.wait("tensor", s_ac, tr_ac)
                    for j in range(g, g + gn):
                        sch.emit("tensor", lambda e, bufw=bufw, j=j:
                                 e.transpose(ps[j % 4][0:128, 0:128],
                                             bufw[:, 128 * j:128 * (j + 1)],
                                             ident[:, :]),
                                 inc=(s_pe if j == g + gn - 1 else None))
                    sch.wait("scalar", s_pe, s_pe.n)
                    for j in range(g, g + gn):
                        sch.emit("scalar", lambda e, j=j, ih=ih: e.copy(
                            woTv(j)[:, 128 * ih:128 * (ih + 1)],
                            ps[j % 4][0:128, 0:128]), inc=s_ac)
                    tr_ac = s_ac.n
                wo_tr[ih] = s_pe.n
            if not obias_done:
                sch.wait("tensor", s_ac, tr_ac)
                for ih in range(DC):
                    for j in range(DC):
                        sch.emit("tensor", lambda e, ih=ih, j=j: e.matmul(
                            ps[ih % 8][0:128, 0:1],
                            woTv(j)[:, 128 * ih:128 * (ih + 1)],
                            bvp[:, j:j + 1],
                            start=(j == 0), stop=(j == DC - 1)),
                            inc=(s_pe if j == DC - 1 else None))
                obpe = s_pe.n
                sch.wait("vector", s_pe, obpe)
                for ih in range(DC):
                    sch.emit("vector", lambda e, ih=ih: e.scalar_tensor_tensor(
                        obias[:, ih:ih + 1], ps[ih % 8][0:128, 0:1], 1.0,
                        bop[:, ih:ih + 1], ALU.mult, ALU.add), inc=s_dv)
                obias_done = True
            obdv = s_dv.n
            ev_prev = tr_ac
            sch.wait("tensor", s_dv, obdv)
            for st in range(NST):
                sch.wait("tensor", s_dv, attev)
                sch.wait("tensor", s_ac, ev_prev)
                for ih in range(DC):
                    bank = ih % 8
                    for j in range(DC):
                        sch.emit("tensor", lambda e, ih=ih, j=j, st=st, bank=bank:
                                 e.matmul(ps[bank][0:128, 0:512],
                                          woTv(j)[:, 128 * ih:128 * (ih + 1)],
                                          attv(j)[:, 512 * st:512 * (st + 1)],
                                          start=(j == 0), stop=(j == DC - 1)),
                                 inc=(s_pe if j == DC - 1 else None))
                prpe = s_pe.n
                sch.wait("scalar", s_pe, prpe)
                for ih in range(DC):
                    sch.emit("scalar", lambda e, ih=ih, st=st: e.activation(
                        attv(ih)[:, 512 * st:512 * (st + 1)],
                        ps[ih % 8][0:128, 0:512], AF.Identity,
                        bias=obias[:, ih:ih + 1]),
                        inc=(s_ac if ih == DC - 1 else None))
                ev_prev = s_ac.n
            oproj_ac = s_ac.n
            oproj_pe = s_pe.n
            if DBG and b == 0:
                sch.wait("sync", s_ac, oproj_ac)
                dma(d_acc, attv(0), s_st)
                sch.wait("tensor", s_st, s_st.n)

            u1pe = 0
            u1bev = 0
            amrow_ac = 0
            for tt in range(NST):
                ts = 512 * tt
                sch.wait("sync", s_pe, u1pe)
                lo, n, off = ts - 1, 514, 0
                if tt == 0:
                    lo, n, off = 0, 513, 1
                if ts + 513 > S:
                    n -= 1
                for dc in range(DC):
                    def ldxq(e, dc=dc, lo=lo, n=n, off=off):
                        return e.dma_start(
                            out=regS16[:, XQS + 514 * dc + off:
                                       XQS + 514 * dc + off + n],
                            in_=xq1_d[dc, :, lo:lo + n])
                    sch.emit("sync", ldxq, inc=s_xq, amt=16)
                dlxq = s_xq.n
                pad_dv = 0
                if tt == 0:
                    sch.wait("vector", s_st, p0_spill)
                    sch.wait("vector", s_pe, bl2pe[-1])
                    for dc in range(DC):
                        sch.emit("vector", lambda e, dc=dc: e.memset(
                            regS16[:, XQS + 514 * dc:XQS + 514 * dc + 1], 0.0),
                            inc=(s_dv if dc == DC - 1 else None))
                    pad_dv = s_dv.n
                if tt == NST - 1:
                    sch.wait("vector", s_pe, u1pe)
                    for dc in range(DC):
                        sch.emit("vector", lambda e, dc=dc: e.memset(
                            regS16[:, XQS + 514 * dc + 513:
                                   XQS + 514 * dc + 514], 0.0),
                            inc=(s_dv if dc == DC - 1 else None))
                    pad_dv = s_dv.n
                sch.wait("tensor", s_ac, u1bev)
                if tt > 0:
                    sch.wait("tensor", s_dv, bl2ev[(tt - 1, DC - 1)])
                else:
                    sch.wait("tensor", s_ac, oproj_ac)
                sch.emit("tensor", lambda e, ts=ts: e.matmul(
                    ps[6][0:128, 0:512], ones[0:1, 0:128],
                    u1row[0:1, ts:ts + 512], start=True, stop=True))
                sch.emit("tensor", lambda e, ts=ts: e.matmul(
                    ps[7][0:128, 0:2], ones[0:1, 0:128],
                    u1row[0:1, ts + 512:ts + 514], start=True, stop=True),
                    inc=s_pe)
                u1bpe = s_pe.n
                sch.wait("scalar", s_pe, u1bpe)
                if tt == 0:
                    sch.wait("scalar", s_pe, oproj_pe)
                else:
                    sch.wait("scalar", s_dv, bl2ev[(tt - 1, DC - 1)])
                sch.emit("scalar", lambda e: e.copy(
                    u1b[:, 0:512], ps[6][0:128, 0:512]))
                sch.emit("scalar", lambda e: e.copy(
                    u1b[:, 512:514], ps[7][0:128, 0:2]), inc=s_ac)
                u1bev = s_ac.n
                t1dl = {}
                sil = {}
                convdone = {}

                def mm_cc(cc):
                    sch.wait("vector", s_ac, sil[cc])
                    if cc == 0:
                        sch.emit("vector", lambda e: e.tensor_copy(
                            accmax[:, :], a2v(0)))
                        sch.emit("vector", lambda e: e.tensor_copy(
                            accmin[:, :], a2v(0)), inc=s_dv)
                    else:
                        sch.emit("vector", lambda e, cc=cc: e.tensor_tensor(
                            accmax[:, :], accmax[:, :], a2v(cc), ALU.max))
                        sch.emit("vector", lambda e, cc=cc: e.tensor_tensor(
                            accmin[:, :], accmin[:, :], a2v(cc), ALU.min),
                            inc=s_dv)
                for cc in range(HC):
                    tb = cc % 2
                    t1buf = regS16[:, T1S[tb]:T1S[tb] + DC * 128]
                    if cc >= 2:
                        sch.wait("sync", s_pe, bl1pe[-(1)] if False else
                                 bl1_mark[(tt, cc - 2)])
                    def ldt1(e, cc=cc, t1buf=t1buf):
                        return e.dma_start(
                            out=t1buf.rearrange("p (n f) -> p n f", f=128),
                            in_=t1t_d[cc, :, :, :].rearrange("n p f -> p n f"))
                    sch.emit("sync", ldt1, inc=s_t1[cc % 2], amt=16)
                    t1dl[cc] = s_t1[cc % 2].n
                    bank = cc % 4
                    tbank = 4 + cc % 2
                    sch.wait("tensor", s_t1[cc % 2], t1dl[cc])
                    if cc == 0:
                        sch.wait("tensor", s_xq, dlxq)
                        sch.wait("tensor", s_ac, u1bev)
                    if cc == 0 and pad_dv:
                        sch.wait("tensor", s_dv, pad_dv)
                    sch.wait("tensor", s_dv, blev.get((tt, cc - 4), 0))
                    sch.wait("tensor", s_dv, blev.get((tt, cc - 2), 0))
                    for dc in range(DC):
                        sch.emit("tensor", lambda e, dc=dc, t1buf=t1buf, bank=bank:
                                 e.matmul(ps[bank][0:128, 0:512],
                                          t1buf[:, 128 * dc:128 * (dc + 1)],
                                          regS16[:, XQS + 514 * dc:
                                                 XQS + 514 * dc + 512],
                                          start=(dc == 0), stop=(dc == DC - 1)))
                        sch.emit("tensor", lambda e, dc=dc, t1buf=t1buf, tbank=tbank:
                                 e.matmul(ps[tbank][0:128, 0:2],
                                          t1buf[:, 128 * dc:128 * (dc + 1)],
                                          regS16[:, XQS + 514 * dc + 512:
                                                 XQS + 514 * dc + 514],
                                          start=(dc == 0), stop=(dc == DC - 1)),
                                 inc=(s_pe if dc == DC - 1 else None))
                    bl1_mark[(tt, cc)] = s_pe.n
                    sch.wait("vector", s_pe, bl1_mark[(tt, cc)])
                    if cc == 0:
                        sch.wait("vector", s_ac, u1bev)
                        if tt == 0:
                            sch.wait("vector", s_pe, oproj_pe)
                    sch.emit("vector", lambda e, bank=bank: e.tensor_tensor(
                        hsb[:, 0:512], ps[bank][0:128, 0:512],
                        u1b[:, 0:512], ALU.mult))
                    sch.emit("vector", lambda e, tbank=tbank: e.tensor_tensor(
                        hsb[:, 512:514], ps[tbank][0:128, 0:2],
                        u1b[:, 512:514], ALU.mult), inc=s_dv)
                    blev[(tt, cc)] = s_dv.n
                    sch.emit("vector", lambda e, cc=cc: e.tensor_scalar(
                        a2v(cc), hsb[:, 0:512], cw0[:, cc:cc + 1],
                        cbt[:, cc:cc + 1], ALU.mult, ALU.add))
                    sch.emit("vector", lambda e, cc=cc: e.scalar_tensor_tensor(
                        a2v(cc), hsb[:, 1:513], cw1[:, cc:cc + 1],
                        a2v(cc), ALU.mult, ALU.add))
                    sch.emit("vector", lambda e, cc=cc: e.scalar_tensor_tensor(
                        a2v(cc), hsb[:, 2:514], cw2[:, cc:cc + 1],
                        a2v(cc), ALU.mult, ALU.add), inc=s_dv)
                    convdone[cc] = s_dv.n
                    sch.wait("scalar", s_dv, convdone[cc])
                    sch.emit("scalar", lambda e, cc=cc: e.activation(
                        a2v(cc), a2v(cc), AF.Silu), inc=s_ac)
                    sil[cc] = s_ac.n
                    if cc >= 2:
                        mm_cc(cc - 2)
                for cc in (HC - 2, HC - 1):
                    mm_cc(cc)
                sch.emit("vector", lambda e: e.tensor_scalar(
                    gsb[:, 0:512], accmin[:, :], -1.0, None, ALU.mult))
                sch.emit("vector", lambda e: e.tensor_tensor(
                    amaxb[:, :], accmax[:, :], gsb[:, 0:512], ALU.max),
                    inc=s_dv)
                amf = s_dv.n
                sch.wait("tensor", s_dv, amf)
                sch.wait("tensor", s_dv, blev[(tt, HC - 1)])
                sch.wait("tensor", s_dv, blev[(tt, HC - 2)])
                sch.wait("tensor", s_ac, s_ac.n)
                for k in range(4):
                    sch.emit("tensor", lambda e, k=k: e.transpose(
                        ps[k][0:128, 0:128], amaxb[:, 128 * k:128 * (k + 1)],
                        ident[:, :]), inc=(s_pe if k == 3 else None))
                trpe = s_pe.n
                sch.wait("vector", s_pe, trpe)
                for k in range(4):
                    sch.emit("vector", lambda e, k=k: e.reduce_max(
                        cols[:, k:k + 1], ps[k][0:128, 0:128], axis=AX.X),
                        inc=(s_dv if k == 3 else None))
                redv = s_dv.n
                sch.wait("tensor", s_dv, redv)
                for k in range(4):
                    sch.emit("tensor", lambda e, k=k: e.transpose(
                        ps[4][0:1, 128 * k:128 * (k + 1)], cols[:, k:k + 1],
                        ident[:, :]), inc=(s_pe if k == 3 else None))
                rowpe = s_pe.n
                sch.wait("scalar", s_pe, rowpe)
                sch.emit("scalar", lambda e: e.copy(
                    zl[0:1, 0:512], ps[4][0:1, 0:512]), inc=s_ac)
                rowev = s_ac.n
                sch.wait("tensor", s_ac, rowev)
                sch.emit("tensor", lambda e: e.matmul(
                    ps[5][0:128, 0:512], ones[0:1, 0:128], zl[0:1, 0:512],
                    start=True, stop=True), inc=s_pe)
                bcpe = s_pe.n
                sch.wait("vector", s_pe, bcpe)
                sch.emit("vector", lambda e: e.tensor_scalar(
                    amaxb[:, :], ps[5][0:128, 0:512], 1e-5, None, ALU.max))
                vspace()
                sch.emit("vector", lambda e: e.reciprocal(
                    s2b[:, :], amaxb[:, :]))
                vspace()
                sch.emit("vector", lambda e: e.tensor_scalar(
                    s2b[:, :], s2b[:, :], 127.0, None, ALU.mult))
                sch.emit("vector", lambda e: e.tensor_scalar(
                    u2b[:, :], amaxb[:, :], mwu[:, 1:2], None, ALU.mult),
                    inc=s_dv)
                sclv = s_dv.n
                gsbq = [gsb[:, 0:512], accmax[:, 0:512]]
                m16q = [m16f[:, 0:512], amaxb.bitcast(FP16)[:, 0:512]]
                hsbq = [hsb[:, 0:512], accmin[:, 0:512]]
                if tt == 0:
                    sch.wait("vector", s_pe, oproj_pe)
                    sch.wait("scalar", s_pe, oproj_pe)
                else:
                    sch.wait("vector", s_pe, bl2pe[-1])
                    sch.wait("scalar", s_pe, bl2pe[-1])
                q2g = {}
                q2h = {}
                q2m16 = {}
                q2cp = {}
                for cc in range(HC):
                    par = cc % 2
                    if cc >= 2:
                        sch.wait("vector", s_ac, q2m16[cc - 2])
                    sch.emit("vector", lambda e, cc=cc, par=par: e.tensor_tensor(
                        gsbq[par], a2v(cc), s2b[:, :], ALU.mult), inc=s_dv)
                    q2g[cc] = s_dv.n
                    sch.wait("scalar", s_dv, q2g[cc])
                    sch.emit("scalar", lambda e, par=par: e.activation(
                        m16q[par], gsbq[par], AF.Identity, bias=MAGIC),
                        inc=s_ac)
                    q2m16[cc] = s_ac.n
                    if cc >= 1:
                        pv = (cc - 1) % 2
                        sch.wait("vector", s_ac, q2m16[cc - 1])
                        sch.emit("vector", lambda e, pv=pv: e.tensor_scalar(
                            hsbq[pv], m16q[pv], -MAGIC, None, ALU.add),
                            inc=s_dv)
                        q2h[cc - 1] = s_dv.n
                        sch.wait("scalar", s_dv, q2h[cc - 1])
                        sch.emit("scalar", lambda e, cc=cc, pv=pv: e.copy(
                            xq2v(cc - 1), hsbq[pv]), inc=s_ac)
                        q2cp[cc - 1] = s_ac.n
                pv = (HC - 1) % 2
                sch.wait("vector", s_ac, q2m16[HC - 1])
                sch.emit("vector", lambda e, pv=pv: e.tensor_scalar(
                    hsbq[pv], m16q[pv], -MAGIC, None, ALU.add), inc=s_dv)
                q2h[HC - 1] = s_dv.n
                sch.wait("scalar", s_dv, q2h[HC - 1])
                sch.emit("scalar", lambda e, pv=pv: e.copy(
                    xq2v(HC - 1), hsbq[pv]), inc=s_ac)
                q2cp[HC - 1] = s_ac.n
                q2dv[(tt, HC - 1)] = s_dv.n
                if DBG and b == 0 and tt == 0:
                    sch.wait("sync", s_dv, q2dv[(tt, HC - 1)])
                    sch.wait("sync", s_ac, q2cp[HC - 1])
                    dma(d_s2b, s2b[:, :], s_st)
                    dma(d_a2, a2v(0), s_st)
                    dma(d_xq2, xq2v(0), s_st)
                    sch.wait("tensor", s_st, s_st.n)
                for ic in range(DC):
                    sch.wait("sync", s_pe, bl2pe[-1])
                    def ldt2(e, ic=ic):
                        return e.dma_start(
                            out=regS16[:, T2S:T2S + HC * 128]
                            .rearrange("p (n f) -> p n f", f=128),
                            in_=t2t_d[ic, :, :, :].rearrange("n p f -> p n f"))
                    sch.emit("sync", ldt2, inc=s_t2, amt=16)
                    t2dl = s_t2.n
                    bank = ic % 8
                    sch.wait("tensor", s_t2, t2dl)
                    sch.wait("tensor", s_dv, q2dv[(tt, HC - 1)])
                    sch.wait("tensor", s_ac, q2cp[HC - 1])
                    sch.wait("tensor", s_dv, bl2ev.get((tt, ic - 8), 0))
                    if ic == 0:
                        sch.wait("tensor", s_dv, redv)
                        sch.wait("tensor", s_ac, rowev)
                    for cc in range(HC):
                        sch.emit("tensor", lambda e, cc=cc, bank=bank:
                                 e.matmul(ps[bank][0:128, 0:512],
                                          regS16[:, T2S + 128 * cc:
                                                 T2S + 128 * (cc + 1)],
                                          xq2v(cc),
                                          start=(cc == 0), stop=(cc == HC - 1)),
                                 inc=(s_pe if cc == HC - 1 else None))
                    bl2pe.append(s_pe.n)
                    sch.wait("vector", s_pe, bl2pe[-1])
                    if ic == 0:
                        sch.wait("vector", s_dv, sclv)
                    sch.emit("vector", lambda e, bank=bank: e.tensor_tensor(
                        gsb[:, 0:512], ps[bank][0:128, 0:512], u2b[:, :],
                        ALU.mult))
                    sch.emit("vector", lambda e, ic=ic, ts=ts: e.tensor_tensor(
                        attv(ic)[:, ts:ts + 512], attv(ic)[:, ts:ts + 512],
                        gsb[:, 0:512], ALU.add), inc=s_dv)
                    bl2ev[(tt, ic)] = s_dv.n
                u1pe = bl1_mark[(tt, HC - 1)]
            ffn_dv = s_dv.n
            regB_dv_free = s_dv.n

            fin_ev = 0
            fin_dvh = {-1: 0, -2: 0}
            fin_soh = {-1: 0, -2: 0}
            for t in range(NT):
                t0 = t * 128
                par = t % 2
                xt2 = regC[:, 0:D] if par == 0 else regC[:, D:2 * D]
                ot = (regC[:, 4 * D:5 * D] if par == 0
                      else regC[:, 5 * D:6 * D])
                pb = 2 * par
                if t == 0:
                    sch.wait("sync", s_pe, bl2pe[-1])
                    sch.wait("sync", s_dv, ffn_dv)
                else:
                    sch.wait("sync", s_dv, fin_dvh[t - 2])
                dma(xt2, x_in[b, t0:t0 + 128, :], s_lx)
                dlf = s_lx.n
                if t == 0:
                    sch.wait("tensor", s_dv, ffn_dv)
                    sch.wait("tensor", s_ac, s_ac.n)
                sch.wait("tensor", s_dv, fin_dvh[t - 2])
                for c in range(DC):
                    sch.emit("tensor", lambda e, c=c, t0=t0, pb=pb: e.transpose(
                        ps[pb + c % 2][0:128,
                                       128 * (c // 2):128 * (c // 2) + 128],
                        attv(c)[:, t0:t0 + 128], ident[:, :]),
                        inc=(s_pe if c == DC - 1 else None))
                fpe = s_pe.n
                sch.wait("vector", s_pe, fpe)
                sch.wait("vector", s_lx, dlf)
                sch.wait("vector", s_so[par], fin_soh[t - 2])
                for c in range(DC):
                    sch.emit("vector", lambda e, c=c, pb=pb, ot=ot, xt2=xt2:
                             e.tensor_tensor(
                                 ot[:, 128 * c:128 * (c + 1)],
                                 ps[pb + c % 2][0:128,
                                                128 * (c // 2):
                                                128 * (c // 2) + 128],
                                 xt2[:, 128 * c:128 * (c + 1)], ALU.add),
                             inc=(s_dv if c == DC - 1 else None))
                fin_ev = s_dv.n
                fin_dvh[t] = fin_ev
                sch.wait("sync", s_dv, fin_ev)
                dma(out_d[b, t0:t0 + 128, :], ot, s_so[par])
                fin_soh[t] = s_so[par].n
                fin_st.append(s_st.n)
            fin_dv = fin_ev
            regA_pe_free = s_pe.n
            regC_pe_free = s_pe.n
            regC_dv_free = fin_ev

        sch.wait("sync", s_st, s_st.n)
        sch.wait("sync", s_so[0], s_so[0].n)
        sch.wait("sync", s_so[1], s_so[1].n)

        block = es.enter_context(nc.Block())
        sch.run(block)
    return nc


bl1_mark = {}


def _fresh_build(cfg):
    global bl1_mark
    bl1_mark = {}
    return build(cfg)


_NC_CACHE = {}


def kernel(L0_attn_x, L0_attn_wq, L0_attn_bq, L0_attn_wk, L0_attn_bk,
           L0_attn_wv, L0_attn_bv, L0_attn_wo, L0_attn_bo,
           L0_ffn_w1, L0_ffn_conv_w, L0_ffn_conv_b, L0_ffn_w2):
    x = np.ascontiguousarray(np.asarray(L0_attn_x, dtype=np.float32))
    Bf, S, D = x.shape
    ncores = 8
    bpc = Bf // ncores
    key = (bpc, S, D, int(np.asarray(L0_attn_wq).shape[0]),
           int(np.asarray(L0_ffn_w1).shape[0]))
    if key not in _NC_CACHE:
        _NC_CACHE[key] = _fresh_build(dict(
            B=key[0], S=key[1], D=key[2], H=key[3], DH=key[4]))
    nc = _NC_CACHE[key]
    w = {k: np.ascontiguousarray(np.asarray(v, dtype=np.float32)) for k, v in
         dict(wq=L0_attn_wq, wk=L0_attn_wk, wv=L0_attn_wv,
              bq=L0_attn_bq, bk=L0_attn_bk, bv=L0_attn_bv,
              wo=L0_attn_wo, bo=L0_attn_bo, w1=L0_ffn_w1,
              cw=L0_ffn_conv_w, cb=L0_ffn_conv_b, w2=L0_ffn_w2).items()}
    in_maps = []
    for c in range(ncores):
        m = {"x": x[c * bpc:(c + 1) * bpc]}
        m.update(w)
        in_maps.append(m)
    res = run_bass_kernel_spmd(nc, in_maps, core_ids=list(range(ncores)))
    return np.concatenate([r["out"] for r in res.results], axis=0)


# version 07 (final)
# speedup vs baseline: 1.0247x; 1.0247x over previous
import contextlib
import math
import numpy as np
import concourse.bass as bass
import concourse.mybir as mybir
from concourse.bass_utils import run_bass_kernel_spmd

F32 = mybir.dt.float32
BF16 = mybir.dt.bfloat16
FP16 = mybir.dt.float16
AF = mybir.ActivationFunctionType
ALU = mybir.AluOpType
AX = mybir.AxisListType

MEXP = 10.0
SCALE = 0.125
MAGIC = 1536.0
ENGINES = ("sync", "tensor", "vector", "scalar", "gpsimd")


class Sem:
    __slots__ = ("h", "n", "name")

    def __init__(self, h, name):
        self.h, self.n, self.name = h, 0, name


class Sched:
    def __init__(self, nc, es):
        self.nc, self.es = nc, es
        self.q = {e: [] for e in ENGINES}

    def sem(self, name):
        return Sem(self.es.enter_context(self.nc.semaphore(name)), name)

    def emit(self, eng, fn, inc=None, amt=1):
        if inc is not None:
            inc.n += amt

            def wrapped(e, fn=fn, h=inc.h, amt=amt):
                fn(e).then_inc(h, amt)
            self.q[eng].append(wrapped)
        else:
            self.q[eng].append(fn)

    def wait(self, eng, sem, val):
        assert val <= sem.n, f"{eng} waits {sem.name}>={val}, scheduled {sem.n}"
        if val > 0:
            self.q[eng].append(lambda e, h=sem.h, v=val: e.wait_ge(h, v))

    def run(self, block):
        for name in ENGINES:
            ops = self.q[name]

            def body(e, ops=ops):
                for op in ops:
                    op(e)
            getattr(block, name)(body)


def build(cfg):
    B, S, D, H, DH = cfg["B"], cfg["S"], cfg["D"], cfg["H"], cfg["DH"]
    E = 64
    NP = H // 2
    DC = D // 128
    HC = DH // 128
    NT = S // 128
    NST = S // 512
    NTC = S // 128
    assert D == H * E and S % 512 == 0 and DH % 128 == 0 and NP == DC

    nc = bass.Bass("TRN2", target_bir_lowering=False,
                   detect_race_conditions=False)

    x_in = nc.dram_tensor("x", [B, S, D], F32, kind="ExternalInput").ap()
    wq_in = nc.dram_tensor("wq", [H, D, E], F32, kind="ExternalInput").ap()
    wk_in = nc.dram_tensor("wk", [H, D, E], F32, kind="ExternalInput").ap()
    wv_in = nc.dram_tensor("wv", [H, D, E], F32, kind="ExternalInput").ap()
    bq_in = nc.dram_tensor("bq", [H, 1, E], F32, kind="ExternalInput").ap()
    bk_in = nc.dram_tensor("bk", [H, 1, E], F32, kind="ExternalInput").ap()
    bv_in = nc.dram_tensor("bv", [H, 1, E], F32, kind="ExternalInput").ap()
    wo_in = nc.dram_tensor("wo", [D, D], F32, kind="ExternalInput").ap()
    bo_in = nc.dram_tensor("bo", [D], F32, kind="ExternalInput").ap()
    w1_in = nc.dram_tensor("w1", [DH, D], F32, kind="ExternalInput").ap()
    cw_in = nc.dram_tensor("cw", [DH, 1, 3], F32, kind="ExternalInput").ap()
    cb_in = nc.dram_tensor("cb", [DH], F32, kind="ExternalInput").ap()
    w2_in = nc.dram_tensor("w2", [D, DH], F32, kind="ExternalInput").ap()
    out_d = nc.dram_tensor("out", [B, S, D], F32, kind="ExternalOutput").ap()
    DBG = cfg.get("debug", False)
    if DBG:
        d_xnf = nc.dram_tensor("d_xnf", [128, S], F32, kind="ExternalOutput").ap()
        d_u1r = nc.dram_tensor("d_u1r", [1, S + 2], F32, kind="ExternalOutput").ap()
        d_qf = nc.dram_tensor("d_qf", [128, S], F32, kind="ExternalOutput").ap()
        d_kf = nc.dram_tensor("d_kf", [128, S], F32, kind="ExternalOutput").ap()
        d_vp = nc.dram_tensor("d_vp", [128, 2048], F32, kind="ExternalOutput").ap()
        d_att = nc.dram_tensor("d_att", [128, S], F32, kind="ExternalOutput").ap()
        d_acc = nc.dram_tensor("d_acc", [128, S], F32, kind="ExternalOutput").ap()
        d_t1t = nc.dram_tensor("d_t1t", [HC, DC, 128, 128], BF16, kind="ExternalOutput").ap()
        d_xq1 = nc.dram_tensor("d_xq1", [DC, 128, S], BF16, kind="ExternalOutput").ap()
        d_s2b = nc.dram_tensor("d_s2b", [128, 512], F32, kind="ExternalOutput").ap()
        d_a2 = nc.dram_tensor("d_a2", [128, 512], F32, kind="ExternalOutput").ap()
        d_xq2 = nc.dram_tensor("d_xq2", [128, 512], BF16, kind="ExternalOutput").ap()
        d_mw = nc.dram_tensor("d_mw", [128, 4], F32, kind="ExternalOutput").ap()
        d_cols = nc.dram_tensor("d_cols", [128, 8], F32, kind="ExternalOutput").ap()
        d_xnt = nc.dram_tensor("d_xnt", [128, D], F32, kind="ExternalOutput").ap()
        d_xq1i = nc.dram_tensor("d_xq1i", [128, D], F32, kind="ExternalOutput").ap()
        d_ident = nc.dram_tensor("d_ident", [128, 128], F32, kind="ExternalOutput").ap()
        d_colh = nc.dram_tensor("d_colh", [128, 16 * NT], F32, kind="ExternalOutput").ap()
    t1t_d = nc.dram_tensor("t1t", [HC, DC, 128, 128], BF16, kind="Internal").ap()
    t2t_d = nc.dram_tensor("t2t", [DC, HC, 128, 128], BF16, kind="Internal").ap()
    xq1_d = nc.dram_tensor("xq1s", [DC, 128, S], BF16, kind="Internal").ap()

    es = contextlib.ExitStack()
    with es:
        def sb(name, shape, dt=F32):
            return es.enter_context(nc.sbuf_tensor(name, shape, dt))

        regA = sb("regA", [128, S * NP])
        regA16 = regA.bitcast(BF16)
        regB = sb("regB", [128, max(S * DC, 512 * HC, 12 * 1024)])
        regB16 = regB.bitcast(FP16)
        regC = sb("regC", [128, 8192])
        regC16 = regC.bitcast(BF16)
        regCf = regC.bitcast(FP16)
        regS = sb("regS", [128, 5152])
        regS16 = regS.bitcast(BF16)

        QF, KF, VP = 0, S, 2 * S
        PT = [2 * S + 2048, 2 * S + 2560, 2 * S + 3072, 2 * S + 3584]
        assert PT[3] + 512 <= 8192
        WQKV = [0, 0]
        XQS = 0
        T1S = [DC * 514 + 16, DC * 514 + 16 + DC * 128]
        T2S = T1S[1] + DC * 128
        assert T2S + HC * 128 <= 2 * 5152
        assert 3 * D <= 5152

        ident = sb("ident", [128, 128])
        ones = sb("ones", [128, 128])
        kc = sb("kc", [128, 8])
        rowi = sb("rowi", [128, 1])
        bqp = sb("bqp", [128, NP])
        bkp = sb("bkp", [128, NP])
        bvp = sb("bvp", [128, DC])
        bop = sb("bop", [128, DC])
        obias = sb("obias", [128, DC])
        cw0 = sb("cw0", [128, HC])
        cw1 = sb("cw1", [128, HC])
        cw2 = sb("cw2", [128, HC])
        cbt = sb("cbt", [128, HC])
        mwc = sb("mwc", [128, 2])
        mwu = sb("mwu", [128, 2])
        wacc = sb("wacc", [128, max(HC, DC)])
        wsum = sb("wsum", [128, 1])
        wsc = sb("wsc", [128, 4])
        cols = sb("cols", [128, 32])
        u1row = sb("u1row", [1, S + 2])
        zbs = sb("zbs", [128, 512])
        hsb = sb("hsb", [128, 514])
        gsb = sb("gsb", [128, 512])
        accmin = sb("accmin", [128, 512])
        amaxb = sb("amaxb", [128, 512])
        s2b = sb("s2b", [128, 512])
        u2b = sb("u2b", [128, 512])
        u1x = sb("u1x", [128, 516])
        coli = gsb[:, 0:128]
        zl = s2b
        accmax = zbs
        u1b = u1x[:, 0:514]
        m16f = u1x.bitcast(FP16)[:, 0:512]
        xqst = u1x.bitcast(BF16)[:, 0:D]
        colh = sb("colh", [128, 16 * (S // 128)]) if cfg.get("debug") else None

        ps = [es.enter_context(nc.psum_tensor(f"ps{i}", [128, 512], F32))
              for i in range(8)]

        sch = Sched(nc, es)
        s_dl = sch.sem("s_dl")
        s_st = sch.sem("s_st")
        s_pe = sch.sem("s_pe")
        s_ac = sch.sem("s_ac")
        s_dv = sch.sem("s_dv")
        s_gp = sch.sem("s_gp")
        s_lx = sch.sem("s_lx")
        s_lq = sch.sem("s_lq")
        s_xq = sch.sem("s_xq")
        s_t2 = sch.sem("s_t2")
        s_pw = [sch.sem("s_pw0"), sch.sem("s_pw1")]
        s_so = [sch.sem("s_so0"), sch.sem("s_so1")]
        s_wo = [sch.sem("s_wo0"), sch.sem("s_wo1")]
        s_t1 = [sch.sem("s_t10"), sch.sem("s_t11")]

        nc.const_aps.register(F32, -MEXP, kc[:, 0:1])
        nc.const_aps.register(F32, SCALE, kc[:, 1:2])
        nc.const_aps.register(F32, -1.0, kc[:, 2:3])
        nc.const_aps.register(F32, 1e-5, kc[:, 3:4])
        SQS = 1.0 / math.sqrt(D)
        assert (1.0 / SQS) == int(1.0 / SQS)
        nc.const_aps.register(F32, SQS, kc[:, 4:5])
        nc.const_aps.register(F32, MAGIC, kc[:, 5:6])

        def vspace(n=8):
            for _ in range(n):
                sch.emit("vector", lambda e: e.tensor_copy(
                    kc[:, 7:8], kc[:, 0:1]))

        def dma(dst, src, inc, slow=False):
            def f(e, dst=dst, src=src, slow=slow):
                return e.dma_start(out=dst, in_=src,
                                   allow_slow_non_contiguous=slow)
            sch.emit("sync", f, inc=inc, amt=16)

        def xnf(c):
            return regB[:, S * c:S * (c + 1)]

        def a2v(c):
            return regB[:, 512 * c:512 * (c + 1)]

        def attv(j):
            return regA[:, S * j:S * (j + 1)]

        def woTv(j):
            return regC[:, 1024 * j:1024 * (j + 1)]

        def xq2v(c):
            return regC16[:, 512 * c:512 * (c + 1)]

        sch.emit("gpsimd", lambda e: e.iota(
            rowi[:, :], [[1, 1]], channel_multiplier=1,
            allow_small_or_imprecise_dtypes=True))
        sch.emit("gpsimd", lambda e: e.iota(
            coli[:, :], [[1, 128]], channel_multiplier=0,
            allow_small_or_imprecise_dtypes=True), inc=s_gp)
        gp_iota = s_gp.n

        sch.emit("vector", lambda e: e.memset(ones[:, :], 1.0))
        sch.emit("vector", lambda e: e.memset(kc[:, 0:1], -MEXP))
        sch.emit("vector", lambda e: e.memset(kc[:, 1:2], SCALE))
        sch.emit("vector", lambda e: e.memset(kc[:, 2:3], -1.0))
        sch.emit("vector", lambda e: e.memset(kc[:, 3:4], 1e-5))
        sch.emit("vector", lambda e: e.memset(kc[:, 4:5], SQS))
        sch.emit("vector", lambda e: e.memset(kc[:, 5:6], MAGIC))
        sch.emit("vector", lambda e: e.memset(u1row[0:1, 0:1], 0.0))
        sch.emit("vector", lambda e: e.memset(u1row[0:1, S + 1:S + 2], 0.0))
        sch.wait("vector", s_gp, gp_iota)
        sch.emit("vector", lambda e: e.tensor_scalar(
            ident[:, :], coli[:, :], rowi[:, 0:1], None, ALU.is_equal),
            inc=s_dv)
        dv_const = s_dv.n

        dma(bqp[:, 0:NP], bass.AP(bq_in.tensor, 0, [[1, 128], [128, NP]]),
            s_dl, slow=True)
        dma(bkp[:, 0:NP], bass.AP(bk_in.tensor, 0, [[1, 128], [128, NP]]),
            s_dl, slow=True)
        dma(bvp[:, 0:DC], bass.AP(bv_in.tensor, 0, [[1, 128], [128, DC]]),
            s_dl, slow=True)
        dma(bop[:, 0:DC], bass.AP(bo_in.tensor, 0, [[1, 128], [128, DC]]),
            s_dl, slow=True)
        for k, t in ((0, cw0), (1, cw1), (2, cw2)):
            dma(t[:, 0:HC], bass.AP(cw_in.tensor, k, [[3, 128], [384, HC]]),
                s_dl, slow=True)
        dma(cbt[:, 0:HC], bass.AP(cb_in.tensor, 0, [[1, 128], [128, HC]]),
            s_dl, slow=True)
        dl_smalls = s_dl.n

        wq16 = regA16

        def quant_weight(w_ap, nch, ncol, mi, scr):
            nblk = ncol // 128
            gq = regB[:, 0:ncol]
            gm16 = regB16[:, 2 * ncol:3 * ncol]
            sb0 = ncol + ncol // 2
            stg = [regB[:, sb0 + ncol * k:sb0 + ncol * (k + 1)]
                   for k in range(2)]
            dvm = []
            for ch in range(nch):
                buf = stg[ch % 2]
                if ch >= 2:
                    sch.wait("sync", s_dv, dvm[ch - 2])
                dma(buf, w_ap[128 * ch:128 * (ch + 1), :], s_pw[ch % 2])
                sch.wait("vector", s_pw[ch % 2], s_pw[ch % 2].n)
                if ch == 0:
                    sch.wait("vector", s_dv, dv_const)
                sch.emit("vector", lambda e, b=buf, ch=ch:
                         e.scalar_tensor_tensor(
                             gq, b, -1.0, b, ALU.mult, ALU.max,
                             accum_out=wacc[:, ch:ch + 1]), inc=s_dv)
                dvm.append(s_dv.n)
            for _sp in range(8):
                sch.emit("vector", lambda e: e.tensor_copy(
                    wsc[:, 2:3], kc[:, 0:1]))
            sch.emit("vector", lambda e, nch=nch: e.reduce_sum(
                wsum[:, 0:1], wacc[:, 0:nch], axis=AX.X), inc=s_dv)
            sch.wait("vector", s_dv, s_dv.n)
            sch.wait("tensor", s_dv, s_dv.n)
            sch.emit("tensor", lambda e: e.matmul(
                ps[0][0:1, 0:1], wsum[:, 0:1], ones[:, 0:1],
                start=True, stop=True), inc=s_pe)
            sch.wait("vector", s_pe, s_pe.n)
            inv_n = 1.0 / (nch * 128 * ncol)
            sch.emit("vector", lambda e, inv_n=inv_n: e.tensor_scalar(
                wsc[0:1, 0:1], ps[0][0:1, 0:1], inv_n, 1e-5,
                ALU.mult, ALU.max), inc=s_dv)
            sch.wait("tensor", s_dv, s_dv.n)
            sch.emit("tensor", lambda e: e.matmul(
                ps[1][0:128, 0:1], ones[0:1, 0:128], wsc[0:1, 0:1],
                start=True, stop=True), inc=s_pe)
            sch.wait("vector", s_pe, s_pe.n)
            vspace()
            sch.emit("vector", lambda e, mi=mi: e.reciprocal(
                mwc[:, mi:mi + 1], ps[1][0:128, 0:1]))
            sch.emit("vector", lambda e, mi=mi: e.tensor_scalar(
                mwu[:, mi:mi + 1], ps[1][0:128, 0:1], 1.0 / 127.0, None,
                ALU.mult))
            vspace()
            sch.emit("vector", lambda e: e.tensor_copy(
                wsc[:, 3:4], mwc[:, 0:1]), inc=s_dv)
            dv_scale = s_dv.n
            dvq = []
            last_st = 0
            for ch in range(nch):
                buf = stg[ch % 2]
                if ch >= 2:
                    sch.wait("sync", s_dv, dvq[ch - 2])
                dma(buf, w_ap[128 * ch:128 * (ch + 1), :], s_pw[ch % 2])
                sch.wait("vector", s_pw[ch % 2], s_pw[ch % 2].n)
                if ch == 0:
                    sch.wait("vector", s_dv, dv_scale)
                sch.emit("vector", lambda e, b=buf, mi=mi: e.tensor_scalar(
                    gm16, b, mwc[:, mi:mi + 1], MAGIC, ALU.mult, ALU.add))
                sch.emit("vector", lambda e: e.tensor_scalar(
                    gq, gm16, -MAGIC, -1.0, ALU.add, ALU.max))
                sch.emit("vector", lambda e, b=buf: e.tensor_scalar(
                    b, gq, 1.0, None, ALU.min), inc=s_dv)
                dvq.append(s_dv.n)
                sch.wait("tensor", s_dv, s_dv.n)
                sch.wait("tensor", s_st, last_st)
                for g in range(0, nblk, 4):
                    gn = min(4, nblk - g)
                    if g > 0:
                        sch.wait("tensor", s_ac, s_ac.n)
                    for blk in range(g, g + gn):
                        sch.emit("tensor", lambda e, b=buf, blk=blk:
                                 e.transpose(ps[blk % 4][0:128, 0:128],
                                             b[:, 128 * blk:128 * (blk + 1)],
                                             ident[:, :]),
                                 inc=(s_pe if blk == g + gn - 1 else None))
                    sch.wait("scalar", s_pe, s_pe.n)
                    for blk in range(g, g + gn):
                        sch.emit("scalar", lambda e, blk=blk: e.copy(
                            wq16[:, 128 * blk:128 * (blk + 1)],
                            ps[blk % 4][0:128, 0:128]), inc=s_ac)
                sch.wait("sync", s_ac, s_ac.n)
                dma(scr[ch, :, :, :].rearrange("n p f -> p n f"),
                    wq16[:, 0:128 * nblk].rearrange("p (n f) -> p n f", f=128),
                    s_st)
                last_st = s_st.n

        quant_weight(w1_in, HC, D, 0, t1t_d)
        quant_weight(w2_in, DC, DH, 1, t2t_d)
        if DBG:
            sch.wait("sync", s_dv, s_dv.n)
            dma(d_mw[:, 0:2], mwc[:, 0:2], s_st)
            dma(d_mw[:, 2:4], mwu[:, 0:2], s_st)
            dma(d_ident, ident[:, :], s_st)
        prep_st = s_st.n
        prep_pe = s_pe.n

        regA_pe_free = 0
        regB_dv_free = s_dv.n
        regC_pe_free = prep_pe
        regC_dv_free = s_dv.n
        obias_done = False
        attev = 0
        zbev = 0
        lnv = 0
        p0_ev = s_ac.n
        oproj_ac = s_ac.n
        bl1pe = [0]
        bl2pe = [0]
        expev = {}
        blev = {}
        bl2ev = {}
        q2dv = {}
        fin_st = [0]
        fin_dv = 0

        for b in range(B):
            xnt = regC[:, 2 * D:3 * D]
            xq1i = regC[:, 3 * D:4 * D]
            m16p = regCf[:, 8 * D:9 * D]
            sch.wait("sync", s_pe, regC_pe_free)
            sch.wait("sync", s_dv, regC_dv_free)
            prev_spill = 0
            prev_pet = 0
            xt_rd = {}
            sq_ac = {}
            c1_rd = {}
            pet_h = {}
            sp_h = {0: 0, -1: 0}
            pend_t = None

            def emit_evacs(te):
                nonlocal p0_ev, prev_spill
                sch.wait("scalar", s_pe, pet_h[te])
                sch.wait("scalar", s_st, sp_h[te - 1])
                if b > 0 and te == 0:
                    sch.wait("scalar", s_dv, regB_dv_free)
                te0 = te * 128
                for c in range(DC):
                    sch.emit("scalar", lambda e, c=c, te0=te0: e.copy(
                        xnf(c)[:, te0:te0 + 128],
                        ps[c % 2][0:128, 128 * (c // 2):128 * (c // 2) + 128]))
                for c in range(DC):
                    sch.emit("scalar", lambda e, c=c: e.copy(
                        xqst[:, 128 * c:128 * (c + 1)],
                        ps[2 + c % 2][0:128,
                                      128 * (c // 2):128 * (c // 2) + 128]),
                        inc=(s_ac if c == DC - 1 else None))
                if te % 4 == 3 or te == NT - 1:
                    base = (te // 4) * 512
                    ncol = 128 * (te % 4 + 1)
                    sch.emit("scalar", lambda e, base=base, ncol=ncol: e.copy(
                        u1row[0:1, 1 + base:1 + base + ncol],
                        ps[4][0:1, 0:ncol]), inc=s_ac)
                p0_ev = s_ac.n
                sch.wait("sync", s_ac, p0_ev)

                def sp_xq(e, te0=te0):
                    outap = bass.AP(xq1_d.tensor, te0,
                                    [[S, 128], [128 * S, DC], [1, 128]])
                    return e.dma_start(
                        out=outap,
                        in_=xqst[:, 0:D].rearrange("p (n f) -> p n f", f=128))
                sch.emit("sync", sp_xq, inc=s_st, amt=16)
                sp_h[te] = s_st.n
                prev_spill = sp_h[te]

            for t in range(NT):
                t0 = t * 128
                co = 16 * (t % 2)
                colv = cols[:, co:co + 16]
                xt = regC[:, 0:D] if t % 2 == 0 else regC[:, 6 * D:7 * D]
                x2t = regC[:, D:2 * D] if t % 2 == 0 else regC[:, 5 * D:6 * D]
                if t >= 2:
                    sch.wait("sync", s_dv, xt_rd[t - 2])
                    sch.wait("sync", s_ac, sq_ac[t - 2])
                dma(xt, x_in[b, t0:t0 + 128, :], s_lx)
                dlx = s_lx.n
                sch.wait("scalar", s_lx, dlx)
                if b == 0 and t == 0:
                    sch.wait("scalar", s_dv, dv_const)
                if t >= 2:
                    sch.wait("scalar", s_dv, c1_rd[t - 2])
                sch.emit("scalar", lambda e, x2t=x2t, xt=xt: e.activation(
                    x2t, xt, AF.Square, scale=SQS), inc=s_ac)
                sq_ac[t] = s_ac.n
                acx = sq_ac[t]
                sch.wait("vector", s_lx, dlx)
                sch.emit("vector", lambda e, colv=colv, xt=xt: e.reduce_sum(
                    colv[:, 0:1], xt, axis=AX.X), inc=s_dv)
                sch.wait("vector", s_dv, s_dv.n)
                sch.emit("vector", lambda e, colv=colv: e.tensor_tensor(
                    colv[:, 3:4], colv[:, 0:1], colv[:, 0:1], ALU.mult))
                sch.wait("vector", s_ac, acx)
                sch.emit("vector", lambda e, colv=colv, x2t=x2t: e.reduce_sum(
                    colv[:, 1:2], x2t, axis=AX.X), inc=s_dv)
                c1_rd[t] = s_dv.n
                sch.wait("vector", s_dv, s_dv.n)
                sch.emit("vector", lambda e, colv=colv: e.tensor_scalar(
                    colv[:, 2:3], colv[:, 0:1], 1.0 / D, None, ALU.mult))
                sch.emit("vector", lambda e, colv=colv: e.scalar_tensor_tensor(
                    colv[:, 4:5], colv[:, 3:4], -1.0 / (D * D), colv[:, 1:2],
                    ALU.mult, ALU.add), inc=s_dv)
                sch.wait("scalar", s_dv, s_dv.n)
                sch.emit("scalar", lambda e, colv=colv: e.activation(
                    colv[:, 5:6], colv[:, 4:5], AF.Sqrt, bias=1e-5), inc=s_ac)
                sqrt_mark = s_ac.n
                if pend_t is not None:
                    emit_evacs(pend_t)
                    pend_t = None
                sch.wait("vector", s_ac, sqrt_mark)
                vspace()
                sch.emit("vector", lambda e, colv=colv: e.reciprocal(
                    colv[:, 6:7], colv[:, 5:6]))
                vspace()
                sch.wait("vector", s_pe, prev_pet)
                sch.emit("vector", lambda e, colv=colv, xt=xt: e.tensor_scalar(
                    xnt, xt, colv[:, 2:3], colv[:, 6:7],
                    ALU.subtract, ALU.mult))
                sch.emit("vector", lambda e, colv=colv: e.reduce_max(
                    colv[:, 8:9], xnt, axis=AX.X, apply_absolute_value=True),
                    inc=s_dv)
                xt_rd[t] = s_dv.n
                sch.wait("vector", s_dv, s_dv.n)
                sch.emit("vector", lambda e, colv=colv: e.tensor_scalar(
                    colv[:, 9:10], colv[:, 8:9], 1e-5, None, ALU.max))
                vspace()
                sch.emit("vector", lambda e, colv=colv: e.reciprocal(
                    colv[:, 10:11], colv[:, 9:10]))
                vspace()
                sch.emit("vector", lambda e, colv=colv: e.tensor_scalar(
                    colv[:, 11:12], colv[:, 10:11], 127.0, None, ALU.mult))
                sch.emit("vector", lambda e, colv=colv: e.tensor_tensor(
                    colv[:, 12:13], colv[:, 9:10], mwu[:, 0:1], ALU.mult))
                vspace(4)
                sch.emit("vector", lambda e, colv=colv: e.tensor_scalar(
                    m16p, xnt, colv[:, 11:12], MAGIC, ALU.mult, ALU.add))
                sch.emit("vector", lambda e: e.tensor_scalar(
                    xq1i, m16p, -MAGIC, None, ALU.add), inc=s_dv)
                dvq1 = s_dv.n
                if DBG and b == 0:
                    sch.emit("vector", lambda e, t=t, colv=colv: e.tensor_copy(
                        colh[:, 16 * t:16 * (t + 1)], colv), inc=s_dv)
                    dvq1 = s_dv.n
                if DBG and b == 0 and t == 0:
                    sch.wait("sync", s_dv, dvq1)
                    dma(d_cols, colv[:, 0:8], s_st)
                    dma(d_xnt, xnt, s_st)
                    dma(d_xq1i, xq1i, s_st)
                    sch.wait("tensor", s_st, s_st.n)
                sch.wait("tensor", s_dv, dvq1)
                if b == 0 and t == 0:
                    sch.wait("tensor", s_st, prep_st)
                if b > 0 and t == 0:
                    sch.wait("tensor", s_dv, fin_dv)
                sch.wait("tensor", s_ac, p0_ev)
                for c in range(DC):
                    sch.emit("tensor", lambda e, c=c: e.transpose(
                        ps[c % 2][0:128, 128 * (c // 2):128 * (c // 2) + 128],
                        xnt[:, 128 * c:128 * (c + 1)], ident[:, :]))
                for c in range(DC):
                    sch.emit("tensor", lambda e, c=c: e.transpose(
                        ps[2 + c % 2][0:128, 128 * (c // 2):128 * (c // 2) + 128],
                        xq1i[:, 128 * c:128 * (c + 1)], ident[:, :]))
                sch.emit("tensor", lambda e, t=t, colv=colv: e.transpose(
                    ps[4][0:1, 128 * (t % 4):128 * (t % 4) + 128],
                    colv[:, 12:13], ident[:, :]), inc=s_pe)
                pet = s_pe.n
                prev_pet = pet
                pet_h[t] = pet
                pend_t = t
            emit_evacs(NT - 1)
            p0_spill = s_st.n
            p0_dv = s_dv.n
            regC_pe_free = s_pe.n
            if DBG and b == 0:
                sch.wait("sync", s_st, p0_spill)
                dma(d_xnf, xnf(0), s_st)
                dma(d_u1r, u1row[0:1, :], s_st)
                dma(d_xq1, xq1_d[:, :, :].rearrange("n p f -> p n f"), s_st)
                dma(d_colh, colh[:, :], s_st)
                dma(d_t1t, t1t_d[:, :, :, :].rearrange("a n p f -> p a n f"),
                    s_st)

            dl_wqkv = {}

            def load_pair(p):
                buf = WQKV[p % 2]
                for wi, w_ap in ((0, wq_in), (1, wk_in), (2, wv_in)):
                    for two in range(2):
                        def ldw(e, p=p, wi=wi, w_ap=w_ap, buf=buf, two=two):
                            inap = bass.AP(w_ap.tensor,
                                           (2 * p + two) * D * E,
                                           [[E, 128], [128 * E, DC], [1, E]])
                            outap = bass.AP(regS, buf + D * wi + 64 * two,
                                            [[5152, 128], [128, DC], [1, 64]])
                            return e.dma_start(out=outap, in_=inap)
                        sch.emit("sync", ldw, inc=s_lq, amt=16)
                dl_wqkv[p] = s_lq.n

            sch.wait("sync", s_pe, bl2pe[-1])
            vpe_hist = {}
            for p in range(NP):
                buf = WQKV[p % 2]
                qk = regS[:, buf:buf + D]
                kk = regS[:, buf + D:buf + 2 * D]
                vk = regS[:, buf + 2 * D:buf + 3 * D]
                if p > 0:
                    sch.wait("sync", s_pe, vpe_hist[p - 1])
                load_pair(p)
                sch.wait("tensor", s_lq, dl_wqkv[p])
                sch.wait("tensor", s_ac, zbev if p > 0 else p0_ev)
                sch.wait("tensor", s_dv, attev if p > 0 else
                         (fin_dv if b > 0 else 0))
                if p == 0 and b > 0:
                    sch.wait("vector", s_pe, regA_pe_free)
                for st in range(NST):
                    bank = st % 4
                    for dc in range(DC):
                        sch.emit("tensor", lambda e, st=st, dc=dc, qk=qk, bank=bank:
                                 e.matmul(ps[bank][0:128, 0:512],
                                          qk[:, 128 * dc:128 * (dc + 1)],
                                          xnf(dc)[:, 512 * st:512 * (st + 1)],
                                          start=(dc == 0), stop=(dc == DC - 1)),
                                 inc=(s_pe if dc == DC - 1 else None))
                qpe = s_pe.n
                for st in range(NST):
                    bank = 4 + st % 4
                    for dc in range(DC):
                        sch.emit("tensor", lambda e, st=st, dc=dc, kk=kk, bank=bank:
                                 e.matmul(ps[bank][0:128, 0:512],
                                          kk[:, 128 * dc:128 * (dc + 1)],
                                          xnf(dc)[:, 512 * st:512 * (st + 1)],
                                          start=(dc == 0), stop=(dc == DC - 1)),
                                 inc=(s_pe if dc == DC - 1 else None))
                kpe = s_pe.n
                sch.wait("scalar", s_pe, qpe)
                if p == 0:
                    sch.wait("scalar", s_dl, dl_smalls)

                for st in range(NST):
                    sch.emit("scalar", lambda e, st=st, p=p: e.activation(
                        regC[:, QF + 512 * st:QF + 512 * (st + 1)],
                        ps[st % 4][0:128, 0:512], AF.Identity,
                        bias=bqp[:, p:p + 1]),
                        inc=(s_ac if st == NST - 1 else None))
                qev = s_ac.n
                sch.wait("scalar", s_pe, kpe)
                for st in range(NST):
                    sch.emit("scalar", lambda e, st=st, p=p: e.activation(
                        regC[:, KF + 512 * st:KF + 512 * (st + 1)],
                        ps[4 + st % 4][0:128, 0:512], AF.Identity,
                        bias=bkp[:, p:p + 1]),
                        inc=(s_ac if st == NST - 1 else None))
                kev = s_ac.n
                sch.wait("tensor", s_ac, qev)
                for tc in range(NTC):
                    bank = tc % 4
                    for dc in range(DC):
                        sch.emit("tensor",
                                 lambda e, tc=tc, dc=dc, vk=vk, bank=bank:
                                 e.matmul(
                                     ps[bank][0:128,
                                              128 * (tc // 4):128 * (tc // 4) + 128],
                                     xnf(dc)[:, 128 * tc:128 * (tc + 1)],
                                     vk[:, 128 * dc:128 * (dc + 1)],
                                     start=(dc == 0), stop=(dc == DC - 1)),
                                 inc=(s_pe if (dc == DC - 1 and tc >= NTC - 4)
                                      else None))
                vpe = s_pe.n
                vpe_hist[p] = vpe
                sch.wait("scalar", s_pe, vpe)
                for bank in range(4):
                    ntc_b = (NTC - 1 - bank) // 4 + 1
                    sch.emit("scalar", lambda e, bank=bank, ntc_b=ntc_b: e.copy(
                        bass.AP(regC, VP + 128 * bank,
                                [[8192, 128], [512, ntc_b], [1, 128]]),
                        ps[bank][0:128, 0:128 * ntc_b]
                        .rearrange("p (n f) -> p n f", f=128)),
                        inc=(s_ac if bank == 3 else None))
                vev = s_ac.n
                for st in range(NST):
                    sch.wait("tensor", s_dv, attev)
                    sch.wait("tensor", s_ac, lnv)
                    for tc in range(NTC):
                        par = tc % 2
                        bA, bB = 2 * par, 2 * par + 1
                        ptA = regC[:, PT[par]:PT[par] + 512]
                        ptB = regC[:, PT[2 + par]:PT[2 + par] + 512]
                        if tc <= 1:
                            g = vev
                            if st > 0:
                                g = max(g, expev[(p, st - 1, NTC - 2 + par)])
                            if par == 1:
                                g = max(g, zbev)
                            sch.wait("tensor", s_ac, g)
                        else:
                            sch.wait("tensor", s_ac, expev[(p, st, tc - 2)])
                        sch.emit("tensor", lambda e, tc=tc, st=st, bA=bA:
                                 e.matmul(ps[bA][0:128, 0:512],
                                          regC[0:64, KF + 128 * tc:KF + 128 * (tc + 1)],
                                          regC[0:64, QF + 512 * st:QF + 512 * (st + 1)],
                                          start=True, stop=True), inc=s_pe)
                        sch.emit("tensor", lambda e, tc=tc, st=st, bB=bB:
                                 e.matmul(ps[bB][0:128, 0:512],
                                          regC[64:128, KF + 128 * tc:KF + 128 * (tc + 1)],
                                          regC[64:128, QF + 512 * st:QF + 512 * (st + 1)],
                                          start=True, stop=True), inc=s_pe)
                        stpe = s_pe.n
                        sch.wait("scalar", s_pe, stpe)
                        sch.emit("scalar", lambda e, bA=bA, ptA=ptA: e.activation(
                            ptA, ps[bA][0:128, 0:512], AF.Exp,
                            bias=-MEXP, scale=SCALE))
                        sch.emit("scalar", lambda e, bB=bB, ptB=ptB: e.activation(
                            ptB, ps[bB][0:128, 0:512], AF.Exp,
                            bias=-MEXP, scale=SCALE), inc=s_ac)
                        expev[(p, st, tc)] = s_ac.n
                        sch.wait("tensor", s_ac, expev[(p, st, tc)])
                        stt = (tc == 0)
                        stp = (tc == NTC - 1)
                        sch.emit("tensor", lambda e, tc=tc, ptA=ptA, stt=stt, stp=stp:
                                 e.matmul(ps[4][0:64, 0:512],
                                          regC[:, VP + 128 * tc:VP + 128 * tc + 64],
                                          ptA, start=stt, stop=stp,
                                          tile_position=(0, 0)),
                                 inc=(s_pe if stp else None))
                        sch.emit("tensor", lambda e, tc=tc, ptB=ptB, stt=stt, stp=stp:
                                 e.matmul(ps[5][64:128, 0:512],
                                          regC[:, VP + 128 * tc + 64:VP + 128 * (tc + 1)],
                                          ptB, start=stt, stop=stp,
                                          tile_position=(0, 64)),
                                 inc=(s_pe if stp else None))
                        sch.emit("tensor", lambda e, ptA=ptA, stt=stt, stp=stp:
                                 e.matmul(ps[6][0:1, 0:512], ones[:, 0:1],
                                          ptA, start=stt, stop=stp,
                                          tile_position=(0, 0)),
                                 inc=(s_pe if stp else None))
                        sch.emit("tensor", lambda e, ptB=ptB, stt=stt, stp=stp:
                                 e.matmul(ps[7][32:33, 0:512], ones[:, 0:1],
                                          ptB, start=stt, stop=stp,
                                          tile_position=(0, 32)),
                                 inc=(s_pe if stp else None))
                    pvz = s_pe.n
                    sch.wait("scalar", s_pe, pvz)
                    sch.emit("scalar", lambda e: e.activation(
                        zl[0:1, 0:512], ps[6][0:1, 0:512], AF.Ln))
                    sch.emit("scalar", lambda e: e.activation(
                        zl[32:33, 0:512], ps[7][32:33, 0:512], AF.Ln),
                        inc=s_ac)
                    lnv = s_ac.n
                    sch.wait("tensor", s_ac, lnv)
                    sch.wait("tensor", s_ac, zbev)
                    sch.emit("tensor", lambda e: e.matmul(
                        ps[2][0:64, 0:512], ones[0:1, 0:64], zl[0:1, 0:512],
                        start=True, stop=True))
                    sch.emit("tensor", lambda e: e.matmul(
                        ps[2][64:128, 0:512], ones[32:33, 0:64],
                        zl[32:33, 0:512], start=True, stop=True,
                        tile_position=(32, 64)), inc=s_pe)
                    zbpe = s_pe.n
                    sch.wait("scalar", s_pe, zbpe)
                    sch.wait("scalar", s_dv, attev)
                    sch.emit("scalar", lambda e: e.activation(
                        zbs[:, 0:512], ps[2][0:128, 0:512], AF.Exp,
                        scale=-1.0), inc=s_ac)
                    zbev = s_ac.n
                    sch.wait("vector", s_ac, zbev)
                    sch.emit("vector", lambda e, p=p, st=st: e.tensor_tensor(
                        attv(p)[0:64, 512 * st:512 * (st + 1)],
                        ps[4][0:64, 0:512], zbs[0:64, 0:512], ALU.mult))
                    sch.emit("vector", lambda e, p=p, st=st: e.tensor_tensor(
                        attv(p)[64:128, 512 * st:512 * (st + 1)],
                        ps[5][64:128, 0:512], zbs[64:128, 0:512], ALU.mult),
                        inc=s_dv)
                    attev = s_dv.n

            if DBG and b == 0:
                sch.wait("sync", s_ac, s_ac.n)
                sch.wait("sync", s_dv, attev)
                dma(d_qf, regC[:, QF:QF + S], s_st)
                dma(d_kf, regC[:, KF:KF + S], s_st)
                dma(d_vp, regC[:, VP:VP + 2048], s_st)
                dma(d_att, attv(0), s_st)
                sch.wait("tensor", s_st, s_st.n)

            wst = [regB[:, 0:D], regB[:, D:2 * D]]
            wo_tr = {}
            tr_ac = 0
            for ih in range(DC):
                bufw = wst[ih % 2]
                sch.wait("sync", s_pe, wo_tr.get(ih - 2, vpe_hist[NP - 1]))
                if ih < 2:
                    sch.wait("sync", s_pe, vpe_hist[NP - 1])
                dma(bufw, wo_in[128 * ih:128 * (ih + 1), :], s_wo[ih % 2])
                sch.wait("tensor", s_wo[ih % 2], s_wo[ih % 2].n)
                if ih == 0:
                    sch.wait("tensor", s_ac, zbev)
                    sch.wait("tensor", s_dv, attev)
                for g in range(0, DC, 4):
                    gn = min(4, DC - g)
                    sch.wait("tensor", s_ac, tr_ac)
                    for j in range(g, g + gn):
                        sch.emit("tensor", lambda e, bufw=bufw, j=j:
                                 e.transpose(ps[j % 4][0:128, 0:128],
                                             bufw[:, 128 * j:128 * (j + 1)],
                                             ident[:, :]),
                                 inc=(s_pe if j == g + gn - 1 else None))
                    sch.wait("scalar", s_pe, s_pe.n)
                    for j in range(g, g + gn):
                        sch.emit("scalar", lambda e, j=j, ih=ih: e.copy(
                            woTv(j)[:, 128 * ih:128 * (ih + 1)],
                            ps[j % 4][0:128, 0:128]), inc=s_ac)
                    tr_ac = s_ac.n
                wo_tr[ih] = s_pe.n
            if not obias_done:
                sch.wait("tensor", s_ac, tr_ac)
                for ih in range(DC):
                    for j in range(DC):
                        sch.emit("tensor", lambda e, ih=ih, j=j: e.matmul(
                            ps[ih % 8][0:128, 0:1],
                            woTv(j)[:, 128 * ih:128 * (ih + 1)],
                            bvp[:, j:j + 1],
                            start=(j == 0), stop=(j == DC - 1)),
                            inc=(s_pe if j == DC - 1 else None))
                obpe = s_pe.n
                sch.wait("vector", s_pe, obpe)
                for ih in range(DC):
                    sch.emit("vector", lambda e, ih=ih: e.scalar_tensor_tensor(
                        obias[:, ih:ih + 1], ps[ih % 8][0:128, 0:1], 1.0,
                        bop[:, ih:ih + 1], ALU.mult, ALU.add), inc=s_dv)
                obias_done = True
            obdv = s_dv.n
            ev_prev = tr_ac
            sch.wait("tensor", s_dv, obdv)
            for st in range(NST):
                sch.wait("tensor", s_dv, attev)
                sch.wait("tensor", s_ac, ev_prev)
                for ih in range(DC):
                    bank = ih % 8
                    for j in range(DC):
                        sch.emit("tensor", lambda e, ih=ih, j=j, st=st, bank=bank:
                                 e.matmul(ps[bank][0:128, 0:512],
                                          woTv(j)[:, 128 * ih:128 * (ih + 1)],
                                          attv(j)[:, 512 * st:512 * (st + 1)],
                                          start=(j == 0), stop=(j == DC - 1)),
                                 inc=(s_pe if j == DC - 1 else None))
                prpe = s_pe.n
                sch.wait("scalar", s_pe, prpe)
                for ih in range(DC):
                    sch.emit("scalar", lambda e, ih=ih, st=st: e.activation(
                        attv(ih)[:, 512 * st:512 * (st + 1)],
                        ps[ih % 8][0:128, 0:512], AF.Identity,
                        bias=obias[:, ih:ih + 1]),
                        inc=(s_ac if ih == DC - 1 else None))
                ev_prev = s_ac.n
            oproj_ac = s_ac.n
            oproj_pe = s_pe.n
            if DBG and b == 0:
                sch.wait("sync", s_ac, oproj_ac)
                dma(d_acc, attv(0), s_st)
                sch.wait("tensor", s_st, s_st.n)

            u1pe = 0
            u1bev = 0
            amrow_ac = 0
            for tt in range(NST):
                ts = 512 * tt
                sch.wait("sync", s_pe, u1pe)
                lo, n, off = ts - 1, 514, 0
                if tt == 0:
                    lo, n, off = 0, 513, 1
                if ts + 513 > S:
                    n -= 1
                for dc in range(DC):
                    def ldxq(e, dc=dc, lo=lo, n=n, off=off):
                        return e.dma_start(
                            out=regS16[:, XQS + 514 * dc + off:
                                       XQS + 514 * dc + off + n],
                            in_=xq1_d[dc, :, lo:lo + n])
                    sch.emit("sync", ldxq, inc=s_xq, amt=16)
                dlxq = s_xq.n
                pad_dv = 0
                if tt == 0:
                    sch.wait("vector", s_st, p0_spill)
                    sch.wait("vector", s_pe, bl2pe[-1])
                    for dc in range(DC):
                        sch.emit("vector", lambda e, dc=dc: e.memset(
                            regS16[:, XQS + 514 * dc:XQS + 514 * dc + 1], 0.0),
                            inc=(s_dv if dc == DC - 1 else None))
                    pad_dv = s_dv.n
                if tt == NST - 1:
                    sch.wait("vector", s_pe, u1pe)
                    for dc in range(DC):
                        sch.emit("vector", lambda e, dc=dc: e.memset(
                            regS16[:, XQS + 514 * dc + 513:
                                   XQS + 514 * dc + 514], 0.0),
                            inc=(s_dv if dc == DC - 1 else None))
                    pad_dv = s_dv.n
                sch.wait("tensor", s_ac, u1bev)
                if tt > 0:
                    sch.wait("tensor", s_dv, bl2ev[(tt - 1, DC - 1)])
                else:
                    sch.wait("tensor", s_ac, oproj_ac)
                sch.emit("tensor", lambda e, ts=ts: e.matmul(
                    ps[6][0:128, 0:512], ones[0:1, 0:128],
                    u1row[0:1, ts:ts + 512], start=True, stop=True))
                sch.emit("tensor", lambda e, ts=ts: e.matmul(
                    ps[7][0:128, 0:2], ones[0:1, 0:128],
                    u1row[0:1, ts + 512:ts + 514], start=True, stop=True),
                    inc=s_pe)
                u1bpe = s_pe.n
                sch.wait("scalar", s_pe, u1bpe)
                if tt == 0:
                    sch.wait("scalar", s_pe, oproj_pe)
                else:
                    sch.wait("scalar", s_dv, bl2ev[(tt - 1, DC - 1)])
                sch.emit("scalar", lambda e: e.copy(
                    u1b[:, 0:512], ps[6][0:128, 0:512]))
                sch.emit("scalar", lambda e: e.copy(
                    u1b[:, 512:514], ps[7][0:128, 0:2]), inc=s_ac)
                u1bev = s_ac.n
                t1dl = {}
                sil = {}
                convdone = {}

                def mm_cc(cc):
                    sch.wait("vector", s_ac, sil[cc])
                    if cc == 0:
                        sch.emit("vector", lambda e: e.tensor_copy(
                            accmax[:, :], a2v(0)))
                        sch.emit("vector", lambda e: e.tensor_copy(
                            accmin[:, :], a2v(0)), inc=s_dv)
                    else:
                        sch.emit("vector", lambda e, cc=cc: e.tensor_tensor(
                            accmax[:, :], accmax[:, :], a2v(cc), ALU.max))
                        sch.emit("vector", lambda e, cc=cc: e.tensor_tensor(
                            accmin[:, :], accmin[:, :], a2v(cc), ALU.min),
                            inc=s_dv)
                for cc in range(HC):
                    tb = cc % 2
                    t1buf = regS16[:, T1S[tb]:T1S[tb] + DC * 128]
                    if cc >= 2:
                        sch.wait("sync", s_pe, bl1pe[-(1)] if False else
                                 bl1_mark[(tt, cc - 2)])
                    def ldt1(e, cc=cc, t1buf=t1buf):
                        return e.dma_start(
                            out=t1buf.rearrange("p (n f) -> p n f", f=128),
                            in_=t1t_d[cc, :, :, :].rearrange("n p f -> p n f"))
                    sch.emit("sync", ldt1, inc=s_t1[cc % 2], amt=16)
                    t1dl[cc] = s_t1[cc % 2].n
                    bank = cc % 4
                    tbank = 4 + cc % 2
                    sch.wait("tensor", s_t1[cc % 2], t1dl[cc])
                    if cc == 0:
                        sch.wait("tensor", s_xq, dlxq)
                        sch.wait("tensor", s_ac, u1bev)
                    if cc == 0 and pad_dv:
                        sch.wait("tensor", s_dv, pad_dv)
                    sch.wait("tensor", s_dv, blev.get((tt, cc - 4), 0))
                    sch.wait("tensor", s_dv, blev.get((tt, cc - 2), 0))
                    for dc in range(DC):
                        sch.emit("tensor", lambda e, dc=dc, t1buf=t1buf, bank=bank:
                                 e.matmul(ps[bank][0:128, 0:512],
                                          t1buf[:, 128 * dc:128 * (dc + 1)],
                                          regS16[:, XQS + 514 * dc:
                                                 XQS + 514 * dc + 512],
                                          start=(dc == 0), stop=(dc == DC - 1)))
                        sch.emit("tensor", lambda e, dc=dc, t1buf=t1buf, tbank=tbank:
                                 e.matmul(ps[tbank][0:128, 0:2],
                                          t1buf[:, 128 * dc:128 * (dc + 1)],
                                          regS16[:, XQS + 514 * dc + 512:
                                                 XQS + 514 * dc + 514],
                                          start=(dc == 0), stop=(dc == DC - 1)),
                                 inc=(s_pe if dc == DC - 1 else None))
                    bl1_mark[(tt, cc)] = s_pe.n
                    sch.wait("vector", s_pe, bl1_mark[(tt, cc)])
                    if cc == 0:
                        sch.wait("vector", s_ac, u1bev)
                        if tt == 0:
                            sch.wait("vector", s_pe, oproj_pe)
                    sch.emit("vector", lambda e, bank=bank: e.tensor_tensor(
                        hsb[:, 0:512], ps[bank][0:128, 0:512],
                        u1b[:, 0:512], ALU.mult))
                    sch.emit("vector", lambda e, tbank=tbank: e.tensor_tensor(
                        hsb[:, 512:514], ps[tbank][0:128, 0:2],
                        u1b[:, 512:514], ALU.mult), inc=s_dv)
                    blev[(tt, cc)] = s_dv.n
                    sch.emit("vector", lambda e, cc=cc: e.tensor_scalar(
                        a2v(cc), hsb[:, 0:512], cw0[:, cc:cc + 1],
                        cbt[:, cc:cc + 1], ALU.mult, ALU.add))
                    sch.emit("vector", lambda e, cc=cc: e.scalar_tensor_tensor(
                        a2v(cc), hsb[:, 1:513], cw1[:, cc:cc + 1],
                        a2v(cc), ALU.mult, ALU.add))
                    sch.emit("vector", lambda e, cc=cc: e.scalar_tensor_tensor(
                        a2v(cc), hsb[:, 2:514], cw2[:, cc:cc + 1],
                        a2v(cc), ALU.mult, ALU.add), inc=s_dv)
                    convdone[cc] = s_dv.n
                    sch.wait("scalar", s_dv, convdone[cc])
                    sch.emit("scalar", lambda e, cc=cc: e.activation(
                        a2v(cc), a2v(cc), AF.Silu), inc=s_ac)
                    sil[cc] = s_ac.n
                    if cc >= 2:
                        mm_cc(cc - 2)
                for cc in (HC - 2, HC - 1):
                    mm_cc(cc)
                sch.emit("vector", lambda e: e.tensor_scalar(
                    gsb[:, 0:512], accmin[:, :], -1.0, None, ALU.mult))
                sch.emit("vector", lambda e: e.tensor_tensor(
                    amaxb[:, :], accmax[:, :], gsb[:, 0:512], ALU.max),
                    inc=s_dv)
                amf = s_dv.n
                sch.wait("tensor", s_dv, amf)
                sch.wait("tensor", s_dv, blev[(tt, HC - 1)])
                sch.wait("tensor", s_dv, blev[(tt, HC - 2)])
                sch.wait("tensor", s_ac, s_ac.n)
                for k in range(4):
                    sch.emit("tensor", lambda e, k=k: e.transpose(
                        ps[k][0:128, 0:128], amaxb[:, 128 * k:128 * (k + 1)],
                        ident[:, :]), inc=(s_pe if k == 3 else None))
                trpe = s_pe.n
                sch.wait("vector", s_pe, trpe)
                for k in range(4):
                    sch.emit("vector", lambda e, k=k: e.reduce_max(
                        cols[:, k:k + 1], ps[k][0:128, 0:128], axis=AX.X),
                        inc=(s_dv if k == 3 else None))
                redv = s_dv.n
                sch.wait("tensor", s_dv, redv)
                for k in range(4):
                    sch.emit("tensor", lambda e, k=k: e.transpose(
                        ps[4][0:1, 128 * k:128 * (k + 1)], cols[:, k:k + 1],
                        ident[:, :]), inc=(s_pe if k == 3 else None))
                rowpe = s_pe.n
                sch.wait("scalar", s_pe, rowpe)
                sch.emit("scalar", lambda e: e.copy(
                    zl[0:1, 0:512], ps[4][0:1, 0:512]), inc=s_ac)
                rowev = s_ac.n
                sch.wait("tensor", s_ac, rowev)
                sch.emit("tensor", lambda e: e.matmul(
                    ps[5][0:128, 0:512], ones[0:1, 0:128], zl[0:1, 0:512],
                    start=True, stop=True), inc=s_pe)
                bcpe = s_pe.n
                sch.wait("vector", s_pe, bcpe)
                sch.emit("vector", lambda e: e.tensor_scalar(
                    amaxb[:, :], ps[5][0:128, 0:512], 1e-5, None, ALU.max))
                vspace()
                sch.emit("vector", lambda e: e.reciprocal(
                    s2b[:, :], amaxb[:, :]))
                vspace()
                sch.emit("vector", lambda e: e.tensor_scalar(
                    s2b[:, :], s2b[:, :], 127.0, None, ALU.mult))
                sch.emit("vector", lambda e: e.tensor_scalar(
                    u2b[:, :], amaxb[:, :], mwu[:, 1:2], None, ALU.mult),
                    inc=s_dv)
                sclv = s_dv.n
                gsbq = [gsb[:, 0:512], accmax[:, 0:512]]
                m16q = [m16f[:, 0:512], amaxb.bitcast(FP16)[:, 0:512]]
                hsbq = [hsb[:, 0:512], accmin[:, 0:512]]
                if tt == 0:
                    sch.wait("vector", s_pe, oproj_pe)
                    sch.wait("scalar", s_pe, oproj_pe)
                else:
                    sch.wait("vector", s_pe, bl2pe[-1])
                    sch.wait("scalar", s_pe, bl2pe[-1])
                q2g = {}
                q2h = {}
                q2m16 = {}
                q2cp = {}
                for cc in range(HC):
                    par = cc % 2
                    if cc >= 2:
                        sch.wait("vector", s_ac, q2m16[cc - 2])
                    sch.emit("vector", lambda e, cc=cc, par=par: e.tensor_tensor(
                        gsbq[par], a2v(cc), s2b[:, :], ALU.mult), inc=s_dv)
                    q2g[cc] = s_dv.n
                    sch.wait("scalar", s_dv, q2g[cc])
                    sch.emit("scalar", lambda e, par=par: e.activation(
                        m16q[par], gsbq[par], AF.Identity, bias=MAGIC),
                        inc=s_ac)
                    q2m16[cc] = s_ac.n
                    if cc >= 1:
                        pv = (cc - 1) % 2
                        sch.wait("vector", s_ac, q2m16[cc - 1])
                        sch.emit("vector", lambda e, pv=pv: e.tensor_scalar(
                            hsbq[pv], m16q[pv], -MAGIC, None, ALU.add),
                            inc=s_dv)
                        q2h[cc - 1] = s_dv.n
                        sch.wait("scalar", s_dv, q2h[cc - 1])
                        sch.emit("scalar", lambda e, cc=cc, pv=pv: e.copy(
                            xq2v(cc - 1), hsbq[pv]), inc=s_ac)
                        q2cp[cc - 1] = s_ac.n
                pv = (HC - 1) % 2
                sch.wait("vector", s_ac, q2m16[HC - 1])
                sch.emit("vector", lambda e, pv=pv: e.tensor_scalar(
                    hsbq[pv], m16q[pv], -MAGIC, None, ALU.add), inc=s_dv)
                q2h[HC - 1] = s_dv.n
                sch.wait("scalar", s_dv, q2h[HC - 1])
                sch.emit("scalar", lambda e, pv=pv: e.copy(
                    xq2v(HC - 1), hsbq[pv]), inc=s_ac)
                q2cp[HC - 1] = s_ac.n
                q2dv[(tt, HC - 1)] = s_dv.n
                if DBG and b == 0 and tt == 0:
                    sch.wait("sync", s_dv, q2dv[(tt, HC - 1)])
                    sch.wait("sync", s_ac, q2cp[HC - 1])
                    dma(d_s2b, s2b[:, :], s_st)
                    dma(d_a2, a2v(0), s_st)
                    dma(d_xq2, xq2v(0), s_st)
                    sch.wait("tensor", s_st, s_st.n)
                for ic in range(DC):
                    sch.wait("sync", s_pe, bl2pe[-1])
                    def ldt2(e, ic=ic):
                        return e.dma_start(
                            out=regS16[:, T2S:T2S + HC * 128]
                            .rearrange("p (n f) -> p n f", f=128),
                            in_=t2t_d[ic, :, :, :].rearrange("n p f -> p n f"))
                    sch.emit("sync", ldt2, inc=s_t2, amt=16)
                    t2dl = s_t2.n
                    bank = ic % 8
                    sch.wait("tensor", s_t2, t2dl)
                    sch.wait("tensor", s_dv, q2dv[(tt, HC - 1)])
                    sch.wait("tensor", s_ac, q2cp[HC - 1])
                    sch.wait("tensor", s_dv, bl2ev.get((tt, ic - 8), 0))
                    if ic == 0:
                        sch.wait("tensor", s_dv, redv)
                        sch.wait("tensor", s_ac, rowev)
                    for cc in range(HC):
                        sch.emit("tensor", lambda e, cc=cc, bank=bank:
                                 e.matmul(ps[bank][0:128, 0:512],
                                          regS16[:, T2S + 128 * cc:
                                                 T2S + 128 * (cc + 1)],
                                          xq2v(cc),
                                          start=(cc == 0), stop=(cc == HC - 1)),
                                 inc=(s_pe if cc == HC - 1 else None))
                    bl2pe.append(s_pe.n)
                    sch.wait("vector", s_pe, bl2pe[-1])
                    if ic == 0:
                        sch.wait("vector", s_dv, sclv)
                    sch.emit("vector", lambda e, bank=bank: e.tensor_tensor(
                        gsb[:, 0:512], ps[bank][0:128, 0:512], u2b[:, :],
                        ALU.mult))
                    sch.emit("vector", lambda e, ic=ic, ts=ts: e.tensor_tensor(
                        attv(ic)[:, ts:ts + 512], attv(ic)[:, ts:ts + 512],
                        gsb[:, 0:512], ALU.add), inc=s_dv)
                    bl2ev[(tt, ic)] = s_dv.n
                u1pe = bl1_mark[(tt, HC - 1)]
            ffn_dv = s_dv.n
            regB_dv_free = s_dv.n

            fin_ev = 0
            fin_dvh = {-1: 0, -2: 0}
            fin_soh = {-1: 0, -2: 0}
            for t in range(NT):
                t0 = t * 128
                par = t % 2
                xt2 = regC[:, 0:D] if par == 0 else regC[:, D:2 * D]
                ot = (regC[:, 4 * D:5 * D] if par == 0
                      else regC[:, 5 * D:6 * D])
                pb = 2 * par
                if t == 0:
                    sch.wait("sync", s_pe, bl2pe[-1])
                    sch.wait("sync", s_dv, ffn_dv)
                else:
                    sch.wait("sync", s_dv, fin_dvh[t - 2])
                dma(xt2, x_in[b, t0:t0 + 128, :], s_lx)
                dlf = s_lx.n
                if t == 0:
                    sch.wait("tensor", s_dv, ffn_dv)
                    sch.wait("tensor", s_ac, s_ac.n)
                sch.wait("tensor", s_dv, fin_dvh[t - 2])
                for c in range(DC):
                    sch.emit("tensor", lambda e, c=c, t0=t0, pb=pb: e.transpose(
                        ps[pb + c % 2][0:128,
                                       128 * (c // 2):128 * (c // 2) + 128],
                        attv(c)[:, t0:t0 + 128], ident[:, :]),
                        inc=(s_pe if c == DC - 1 else None))
                fpe = s_pe.n
                sch.wait("vector", s_pe, fpe)
                sch.wait("vector", s_lx, dlf)
                sch.wait("vector", s_so[par], fin_soh[t - 2])
                for c in range(DC):
                    sch.emit("vector", lambda e, c=c, pb=pb, ot=ot, xt2=xt2:
                             e.tensor_tensor(
                                 ot[:, 128 * c:128 * (c + 1)],
                                 ps[pb + c % 2][0:128,
                                                128 * (c // 2):
                                                128 * (c // 2) + 128],
                                 xt2[:, 128 * c:128 * (c + 1)], ALU.add),
                             inc=(s_dv if c == DC - 1 else None))
                fin_ev = s_dv.n
                fin_dvh[t] = fin_ev
                sch.wait("sync", s_dv, fin_ev)
                dma(out_d[b, t0:t0 + 128, :], ot, s_so[par])
                fin_soh[t] = s_so[par].n
                fin_st.append(s_st.n)
            fin_dv = fin_ev
            regA_pe_free = s_pe.n
            regC_pe_free = s_pe.n
            regC_dv_free = fin_ev

        sch.wait("sync", s_st, s_st.n)
        sch.wait("sync", s_so[0], s_so[0].n)
        sch.wait("sync", s_so[1], s_so[1].n)

        block = es.enter_context(nc.Block())
        sch.run(block)
    return nc


bl1_mark = {}


def _fresh_build(cfg):
    global bl1_mark
    bl1_mark = {}
    return build(cfg)


_NC_CACHE = {}


def kernel(L0_attn_x, L0_attn_wq, L0_attn_bq, L0_attn_wk, L0_attn_bk,
           L0_attn_wv, L0_attn_bv, L0_attn_wo, L0_attn_bo,
           L0_ffn_w1, L0_ffn_conv_w, L0_ffn_conv_b, L0_ffn_w2):
    x = np.ascontiguousarray(np.asarray(L0_attn_x, dtype=np.float32))
    Bf, S, D = x.shape
    ncores = 8
    bpc = Bf // ncores
    key = (bpc, S, D, int(np.asarray(L0_attn_wq).shape[0]),
           int(np.asarray(L0_ffn_w1).shape[0]))
    if key not in _NC_CACHE:
        _NC_CACHE[key] = _fresh_build(dict(
            B=key[0], S=key[1], D=key[2], H=key[3], DH=key[4]))
    nc = _NC_CACHE[key]
    w = {k: np.ascontiguousarray(np.asarray(v, dtype=np.float32)) for k, v in
         dict(wq=L0_attn_wq, wk=L0_attn_wk, wv=L0_attn_wv,
              bq=L0_attn_bq, bk=L0_attn_bk, bv=L0_attn_bv,
              wo=L0_attn_wo, bo=L0_attn_bo, w1=L0_ffn_w1,
              cw=L0_ffn_conv_w, cb=L0_ffn_conv_b, w2=L0_ffn_w2).items()}
    in_maps = []
    for c in range(ncores):
        m = {"x": x[c * bpc:(c + 1) * bpc]}
        m.update(w)
        in_maps.append(m)
    res = run_bass_kernel_spmd(nc, in_maps, core_ids=list(range(ncores)))
    return np.concatenate([r["out"] for r in res.results], axis=0)
